# version 01 (final)
# baseline (speedup 1.0000x reference)
import numpy as np
import ml_dtypes
import concourse.bass as bass
import concourse.mybir as mybir
from concourse.bass_utils import run_bass_kernel_spmd

F32 = mybir.dt.float32
F32R = mybir.dt.float32r
BF16 = mybir.dt.bfloat16
AF = mybir.ActivationFunctionType
ALU = mybir.AluOpType

B, T, D, H, DK = 8, 2048, 1024, 16, 64
KT = D // 128
DBG_Z = False
TRUNC_B = None
SKIP_B = set()
NBLK = 32
LN_EPS = 1e-5


class Prog:
    CE = ("tensor", "vector", "scalar", "gpsimd")

    def __init__(self, nc):
        self.nc = nc
        self.steps = {e: [] for e in ("sync",) + Prog.CE}
        self._cms = []
        self.eng_sem = {}
        for e in Prog.CE:
            cm = nc.semaphore(f"S_{e}")
            self._cms.append(cm)
            self.eng_sem[e] = [cm.__enter__(), 0]
        self.dma_sems = {}

    def _ch(self, name):
        if name not in self.dma_sems:
            cm = self.nc.semaphore(f"D_{name}")
            self._cms.append(cm)
            self.dma_sems[name] = [cm.__enter__(), 0]
        return self.dma_sems[name]

    def op(self, engine, fn, deps=()):
        sem = self.eng_sem[engine]
        sem[1] += 1
        self.steps[engine].append((list(deps), fn, sem[0], 1))
        return (engine, sem[0], sem[1])

    def dma(self, ch, fn, deps=(), n=1):
        sem = self._ch(ch)
        sem[1] += 16 * n
        self.steps["sync"].append((list(deps), fn, sem[0], 16))
        return ("sync", sem[0], sem[1])

    def emit(self):
        waited = {e: {} for e in self.steps}

        def run(eng_name, eng):
            w = waited[eng_name]
            for deps, fn, sem, amt in self.steps[eng_name]:
                need = {}
                for dv in deps:
                    (src_eng, dsem, val) = dv() if callable(dv) else dv
                    key = id(dsem)
                    if w.get(key, -1) >= val:
                        continue
                    if need.get(key, (None, -1))[1] < val:
                        need[key] = (dsem, val)
                for key, (dsem, val) in need.items():
                    eng.wait_ge(dsem, val)
                    w[key] = val
                ins = fn(eng)
                if ins is None:
                    ins = eng.engine_nop()
                if isinstance(ins, list):
                    for i_ in ins:
                        i_.then_inc(sem, amt)
                else:
                    ins.then_inc(sem, amt)

        with self.nc.Block() as blk:
            blk.sync(lambda s: run("sync", s))
            blk.tensor(lambda t: run("tensor", t))
            blk.vector(lambda v: run("vector", v))
            blk.scalar(lambda a: run("scalar", a))
            blk.gpsimd(lambda g: run("gpsimd", g))
        return blk


def carve(arena, byte_off, n, dtype):
    esz = {F32: 4, F32R: 4, BF16: 2}[dtype]
    assert byte_off % 4 == 0
    f = byte_off // 4
    v = arena[:, f:f + (n * esz + 3) // 4]
    if dtype != F32:
        v = v.bitcast(dtype)
    return v[:, 0:n]



def build_attn_a():
    nc = bass.Bass("TRN2", target_bir_lowering=False, debug=False)
    d = {}
    for nm, shp in [("x", [D, T]), ("wqkvT", [D, 3 * D]), ("wvgsT", [D, 16]),
                    ("woT", [D, D]), ("wcols", [128, 512]), ("wji", [64, 1024]),
                    ("prow", [1, 96]), ("psc", [1, 2])]:
        d[nm] = nc.dram_tensor(nm, shp, F32, kind="ExternalInput").ap()
    d["x1"] = nc.dram_tensor("x1", [D, T], F32, kind="ExternalOutput").ap()
    for nm in ("q_hbm", "kt_hbm", "v_hbm", "y_hbm"):
        d[nm] = nc.dram_tensor(nm, [D, T], BF16).ap()
    d["rows_hbm"] = nc.dram_tensor("rows_hbm", [64, T], F32).ap()
    sel_np = np.zeros((128, NBLK, 64), dtype=ml_dtypes.bfloat16)
    for b_ in range(NBLK):
        for p_ in range(128):
            sel_np[p_, b_, 2 * b_ + p_ // 64] = 1.0
    d["sel"] = nc.inline_tensor(np.ascontiguousarray(sel_np.reshape(128, NBLK * 64)), "sel_c").ap()
    emit_attn_a(nc, d)
    return nc


def emit_attn_a(nc, d):
    p = Prog(nc)
    AR1 = nc.alloc_sbuf_tensor("AR1", [128, 16384], F32).ap()
    AR2 = nc.alloc_sbuf_tensor("AR2", [128, 16384], F32).ap()
    AR3 = nc.alloc_sbuf_tensor("AR3", [128, 16384], F32).ap()
    SM = nc.alloc_sbuf_tensor("SM", [128, 4000], F32).ap()
    PS = nc.alloc_psum_tensor("PS", [128, 4096], F32).ap()

    def bank(i):
        return PS[:, i * 512:(i + 1) * 512]

    off = [0]
    def sm(n, dtype=F32):
        v = carve(SM, off[0], n, dtype)
        off[0] += ((n * (2 if dtype == BF16 else 4) + 3) // 4) * 4
        return v
    wcols_s = sm(512)
    ewcols = sm(512)
    pcolB = sm(16)
    pcolA = sm(16)
    pcolM = sm(4)
    sel_s = sm(2048, BF16)
    wjib = sm(1024, BF16)
    onescol = sm(4)
    onescol_r = sm(4, BF16)
    selh = sm(2)
    selh_r = sm(2, BF16)
    prs = sm(160)
    rowsc = sm(16)
    psc_s = sm(8)
    gmbgT = sm(8)
    bvT = sm(4)
    wv_r = sm(128, BF16)
    c_eps = sm(4)
    c_mhalf = sm(4)
    assert off[0] <= 4000 * 4

    wji_s = carve(AR3, 49152, 1024, F32)
    wv_stage = carve(AR3, 53248, 128, F32)

    xs = AR1
    t_x = p.dma("xin", lambda s: s.dma_start(
        out=xs.rearrange("p (kt t) -> p kt t", kt=KT),
        in_=d["x"].rearrange("(kt p) t -> p kt t", p=128)))
    t_small = [
        p.dma("sml", lambda s: s.dma_start(out=prs[0:1, 0:96], in_=d["prow"])),
        p.dma("sml", lambda s: s.dma_start(out=psc_s[0:1, 0:2], in_=d["psc"])),
        p.dma("sml", lambda s: s.dma_start(out=wcols_s, in_=d["wcols"])),
        p.dma("sml", lambda s: s.dma_start(out=wji_s[0:64, :], in_=d["wji"])),
        p.dma("sml", lambda s: s.dma_start(out=sel_s, in_=d["sel"])),
        p.dma("sml", lambda s: s.dma_start(out=wv_stage.rearrange("p (kt m) -> p kt m", kt=KT),
                                           in_=d["wvgsT"].rearrange("(kt p) m -> p kt m", p=128))),
    ]

    tp1 = p.op("vector", lambda ve: ve.reciprocal(rowsc[0:1, 0:16], prs[0:1, 32:48]),
               deps=t_small)
    tp2 = p.op("vector", lambda ve: ve.tensor_tensor(
        prs[0:1, 96:112], prs[0:1, 0:16], rowsc[0:1, 0:16], ALU.mult), deps=[tp1])
    tp3 = p.op("vector", lambda ve: ve.tensor_scalar_mul(
        prs[0:1, 96:112], prs[0:1, 96:112], -1.0), deps=[tp2])
    tp4 = p.op("vector", lambda ve: ve.tensor_single_scalar(
        prs[0:1, 112:128], prs[0:1, 0:16], psc_s[0:1, 0:1], ALU.mult), deps=[tp1])
    tp5 = p.op("vector", lambda ve: ve.tensor_tensor(
        prs[0:1, 128:144], prs[0:1, 48:64], prs[0:1, 16:32], ALU.mult), deps=[tp1])
    tp6 = p.op("vector", lambda ve: ve.tensor_sub(
        prs[0:1, 144:160], prs[0:1, 80:96], prs[0:1, 64:80]), deps=[tp1])
    tp7 = p.op("vector", lambda ve: ve.tensor_single_scalar(
        psc_s[0:1, 3:4], psc_s[0:1, 1:2], psc_s[0:1, 0:1], ALU.mult), deps=[tp1])
    tp8 = p.op("vector", lambda ve: ve.tensor_scalar_mul(
        psc_s[0:1, 2:3], psc_s[0:1, 3:4], -1.0), deps=[tp7])
    def f_consts(ve):
        ve.memset(c_eps[:, 0:1], 1e-5)
        ve.memset(c_mhalf[:, 0:1], -0.5)
        ve.memset(onescol[:, 0:1], 1.0)
        ve.memset(selh[:, 0:2], 0.0)
        return None
    tp9 = p.op("vector", f_consts, deps=[])
    def f_consts2(ve):
        ve.memset(selh[0:64, 0:1], 1.0)
        ve.memset(selh[64:128, 1:2], 1.0)
        return ve.tensor_copy(onescol_r[:, 0:1], onescol[:, 0:1])
    tp10 = p.op("vector", f_consts2, deps=[tp9])
    tp11 = p.op("vector", lambda ve: ve.tensor_copy(selh_r, selh), deps=[tp10])
    tp12 = p.op("vector", lambda ve: ve.tensor_copy(wjib[0:64, :], wji_s[0:64, :]),
                deps=t_small)
    tp13 = p.op("vector", lambda ve: ve.tensor_copy(wv_r, wv_stage), deps=t_small)
    t_par = p.op("vector", lambda ve: None,
                 deps=[tp3, tp4, tp5, tp6, tp8, tp11, tp12, tp13])

    t_prw = p.dma("rows", lambda s: s.dma_start(out=d["rows_hbm"][0:1, 0:160],
                                                in_=prs[0:1, 0:160]), deps=[t_par])
    t_psc2 = p.dma("rows", lambda s: s.dma_start(out=d["rows_hbm"][1:2, 0:4],
                                                 in_=psc_s[0:1, 0:4]), deps=[t_par])
    t_pcolB = p.dma("sml2", lambda s: s.dma_start(
        out=pcolB, in_=d["rows_hbm"][0:1, 96:112].broadcast_to((128, 16))), deps=[t_prw, t_psc2])
    t_pcolA = p.dma("sml2", lambda s: s.dma_start(
        out=pcolA, in_=d["rows_hbm"][0:1, 112:128].broadcast_to((128, 16))), deps=[t_prw, t_psc2])
    t_gm = p.dma("sml2", lambda s: s.dma_start(
        out=gmbgT[0:2, 0:8], in_=d["rows_hbm"][0:1, 128:144].rearrange("o (k two) -> two k", two=2),
        allow_slow_non_contiguous=True),
        deps=[t_prw, t_psc2])
    t_bv = p.dma("sml2", lambda s: s.dma_start(
        out=bvT[0:16, 0:1], in_=d["rows_hbm"][0:1, 144:160].rearrange("o h -> h o"),
        allow_slow_non_contiguous=True),
        deps=[t_prw, t_psc2])
    t_pcolM = p.dma("sml2", lambda s: s.dma_start(
        out=pcolM[:, 0:1], in_=d["rows_hbm"][1:2, 2:3].broadcast_to((128, 1))),
        deps=[t_prw, t_psc2])
    t_cols = t_pcolM

    t_ew = p.op("scalar", lambda a: a.activation(out=ewcols, in_=wcols_s, func=AF.Exp),
                deps=t_small)

    xr = AR2.bitcast(BF16)[:, 0:KT * T]
    sqring = [carve(AR3, 0, T, BF16), carve(AR3, 8192, T, BF16)]
    strow = carve(AR3, 16384, 4 * T, F32)
    mu_b = carve(AR3, 24576, T, F32)
    rstd_b = carve(AR3, 32768, T, F32)
    s_sb = carve(AR3, 40960, T, F32)
    es_sb = carve(AR3, 49152, T, F32)
    t_xr, t_sq, t_mmx, t_mmq = [], [], [], []
    for k in range(KT):
        t_xr.append(p.op("vector", lambda ve, k=k: ve.tensor_copy(
            xr[:, k * T:(k + 1) * T], xs[:, k * T:(k + 1) * T]), deps=[t_x]))
        dep = [t_x] + ([t_mmq[k - 2]] if k >= 2 else [])
        t_sq.append(p.op("scalar", lambda a, k=k: a.activation(
            out=sqring[k % 2], in_=xs[:, k * T:(k + 1) * T], func=AF.Square), deps=dep))
        def f_sx(te, k=k):
            last = None
            for n in range(4):
                last = te.matmul(PS[0:1, n * 512:(n + 1) * 512], onescol_r[:, 0:1],
                                 xr[:, k * T + n * 512:k * T + (n + 1) * 512],
                                 start=(k == 0), stop=(k == KT - 1))
            return last
        t_mmx.append(p.op("tensor", f_sx, deps=[t_xr[k], t_par]))
        def f_sq2(te, k=k):
            last = None
            for n in range(4):
                last = te.matmul(PS[0:1, (4 + n) * 512:(5 + n) * 512], onescol_r[:, 0:1],
                                 sqring[k % 2][:, n * 512:(n + 1) * 512],
                                 start=(k == 0), stop=(k == KT - 1))
            return last
        t_mmq.append(p.op("tensor", f_sq2, deps=[t_sq[k]]))

    MU, VA, EX, RS = 0, T, 2 * T, 3 * T
    t_st1 = p.op("vector", lambda ve: ve.tensor_scalar_mul(
        strow[0:1, MU:MU + T], PS[0:1, 0:2048], 1.0 / D), deps=[t_mmx[-1], t_mmq[-1]])
    t_st2 = p.op("vector", lambda ve: ve.tensor_scalar_mul(
        strow[0:1, EX:EX + T], PS[0:1, 2048:4096], 1.0 / D), deps=[t_mmx[-1], t_mmq[-1]])
    t_st3 = p.op("vector", lambda ve: ve.tensor_tensor(
        strow[0:1, VA:VA + T], strow[0:1, MU:MU + T], strow[0:1, MU:MU + T], ALU.mult),
        deps=[t_st1])
    t_stats = p.op("vector", lambda ve: ve.tensor_sub(
        strow[0:1, VA:VA + T], strow[0:1, EX:EX + T], strow[0:1, VA:VA + T]),
        deps=[t_st2, t_st3])
    t_rs1 = p.op("scalar", lambda a: a.activation(
        out=strow[0:1, RS:RS + T], in_=strow[0:1, VA:VA + T], func=AF.Ln,
        bias=c_eps[0:1, 0:1]), deps=[t_stats, t_par])
    t_rstd = p.op("scalar", lambda a: a.activation(
        out=strow[0:1, RS:RS + T], in_=strow[0:1, RS:RS + T], func=AF.Exp,
        scale=c_mhalf[0:1, 0:1]), deps=[t_rs1])
    t_rw1 = p.dma("rows", lambda s: s.dma_start(out=d["rows_hbm"][16:17, :],
                                                in_=strow[0:1, MU:MU + T]), deps=[t_stats])
    t_rw2 = p.dma("rows", lambda s: s.dma_start(out=d["rows_hbm"][17:18, :],
                                                in_=strow[0:1, RS:RS + T]), deps=[t_rstd])
    t_mub = p.dma("mub", lambda s: s.dma_start(
        out=mu_b, in_=d["rows_hbm"][16:17, :].broadcast_to((128, T))), deps=[t_rw1, t_rw2])
    t_rsb = p.dma("rsb", lambda s: s.dma_start(
        out=rstd_b, in_=d["rows_hbm"][17:18, :].broadcast_to((128, T))), deps=[t_rw1, t_rw2])
    xn = AR2.bitcast(BF16)[:, 0:KT * T]
    nrm_scr = carve(AR3, 57344, T, F32)
    t_xn = []
    t_nrm_a = None
    for k in range(KT):
        dep = [t_mub, t_rsb, t_mmq[k], t_mmx[k]]
        if t_xn:
            dep.append(t_xn[-1])
        t_nrm_a = p.op("vector", lambda ve, k=k: ve.tensor_tensor(
            nrm_scr, xs[:, k * T:(k + 1) * T], mu_b, ALU.subtract), deps=dep)
        t_xn.append(p.op("vector", lambda ve, k=k: ve.tensor_tensor(
            xn[:, k * T:(k + 1) * T], nrm_scr, rstd_b, ALU.mult), deps=[t_nrm_a]))

    t_smm = []
    for k in range(KT):
        def f_smm(te, k=k):
            last = None
            for n in range(4):
                last = te.matmul(PS[0:16, n * 512:(n + 1) * 512], wv_r[:, k * 16:(k + 1) * 16],
                                 xn[:, k * T + n * 512:k * T + (n + 1) * 512],
                                 start=(k == 0), stop=(k == KT - 1))
            return last
        t_smm.append(p.op("tensor", f_smm, deps=[t_xn[k]]))
    t_srow = p.op("vector", lambda ve: ve.tensor_scalar(
        out=s_sb[0:16, :], in0=PS[0:16, 0:2048], scalar1=bvT[0:16, 0:1], scalar2=None,
        op0=ALU.add), deps=[t_smm[-1], t_cols])
    t_esrow = p.op("scalar", lambda a: a.activation(out=es_sb[0:16, :], in_=s_sb[0:16, :],
                                                    func=AF.Exp), deps=[t_srow])
    t_sbounce = p.dma("rows", lambda s: s.dma_start(out=d["rows_hbm"][32:48, :],
                                                    in_=s_sb[0:16, :]), deps=[t_srow])
    t_esbounce = p.dma("rows", lambda s: s.dma_start(out=d["rows_hbm"][48:64, :],
                                                     in_=es_sb[0:16, :]), deps=[t_esrow])

    wstage = [carve(AR1, 0, 1024, F32), carve(AR1, 4096, 1024, F32)]
    wrnd = [carve(AR1, 8192, 1024, BF16), carve(AR1, 10240, 1024, BF16)]
    evb = [carve(AR1, 32768, T, BF16), carve(AR1, 36864, T, BF16)]
    k2t = carve(AR1, 40960, T, BF16)
    kscr = carve(AR1, 16384, T, F32)
    kmu_b = carve(AR1, 49152, T, F32)
    krs_b = carve(AR1, 57344, T, F32)
    kstrow = carve(AR3, 16384, 4 * T, F32)
    ksq32 = carve(AR3, 0, T, F32)
    t_wld, t_wrd, t_qmm, t_evq, t_spill = [], [], [], [], []
    t_krow = {}
    t_ktil = {}
    t_kmu = {}
    t_krs = {}
    t_kb = {}
    for m in range(24):
        kind = "q" if m < 8 else ("k" if m < 16 else "v")
        bs = 4 * (m % 2)
        dep = [t_xn[-1]] if m < 2 else [t_wrd[m - 2]]
        t_wld.append(p.dma(f"wld{m % 2}", lambda s, m=m: s.dma_start(
            out=wstage[m % 2].rearrange("p (kt c) -> p kt c", kt=KT),
            in_=d["wqkvT"].rearrange("(kt p) c -> p kt c", p=128)[:, :, m * 128:(m + 1) * 128]),
            deps=dep))
        dep = [t_wld[m]] + ([t_qmm[m - 2]] if m >= 2 else [])
        t_wrd.append(p.op("vector", lambda ve, m=m: ve.tensor_copy(wrnd[m % 2], wstage[m % 2]),
                          deps=dep))
        if m == 0:
            pdep = [t_srow]
        elif m == 1:
            pdep = [t_stats]
        else:
            mprev = m - 2
            pdep = [t_evq[mprev]]
            if 8 <= mprev < 16:
                pdep = [t_krow[mprev]]
        def f_qmm(te, m=m, bs=bs):
            last = None
            for n in range(4):
                for k in range(KT):
                    last = te.matmul(bank(bs + n), wrnd[m % 2][:, k * 128:(k + 1) * 128],
                                     xn[:, k * T + n * 512:k * T + (n + 1) * 512],
                                     start=(k == 0), stop=(k == KT - 1))
            return last
        t_qmm.append(p.op("tensor", f_qmm, deps=[t_wrd[m]] + pdep))
        if kind in ("q", "v"):
            def f_ev(ve, m=m, bs=bs):
                last = None
                for n in range(4):
                    last = ve.tensor_copy(evb[m % 2][:, n * 512:(n + 1) * 512], bank(bs + n))
                return last
            dep = [t_qmm[m]] + ([t_spill[m - 2]] if m >= 2 else [])
            t_evq.append(p.op("vector", f_ev, deps=dep))
            tgt = d["q_hbm"] if kind == "q" else d["v_hbm"]
            r0 = (m if m < 8 else m - 16) * 128
            t_spill.append(p.dma(f"spl{m % 2}", lambda s, m=m, tgt=tgt, r0=r0: s.dma_start(
                out=tgt[r0:r0 + 128, :], in_=evb[m % 2]), deps=[t_evq[m]]))
        else:
            kr = m - 8
            def f_evk(ve, m=m, bs=bs):
                last = None
                for n in range(4):
                    last = ve.tensor_copy(evb[m % 2][:, n * 512:(n + 1) * 512], bank(bs + n))
                return last
            dep = [t_qmm[m]] + ([t_spill[m - 2]] if m >= 2 else [])
            t_evq.append(p.op("vector", f_evk, deps=dep))
            dep = [t_evq[m]] + ([t_ktil[m - 2]] if kr >= 2 else [])
            t_k2 = p.op("scalar", lambda a, m=m: a.activation(
                out=k2t, in_=evb[m % 2], func=AF.Square), deps=dep)
            def f_ks(te, m=m, bs=bs):
                last = None
                for n in range(4):
                    last = te.matmul(PS[0:2, (bs + n) * 512:(bs + n + 1) * 512], selh_r,
                                     evb[m % 2][:, n * 512:(n + 1) * 512], start=True, stop=True)
                for n in range(4):
                    last = te.matmul(PS[32:34, (bs + n) * 512:(bs + n + 1) * 512], selh_r,
                                     k2t[:, n * 512:(n + 1) * 512], start=True, stop=True)
                return last
            t_ksmm = p.op("tensor", f_ks, deps=[t_k2, t_evq[m], t_par])
            def f_krowA(ve, m=m, bs=bs):
                ve.tensor_scalar_mul(kstrow[0:2, 0:T], PS[0:2, bs * 512:(bs + 4) * 512], 1.0 / 64)
                return ve.tensor_scalar_mul(ksq32[32:34, 0:T],
                                            PS[32:34, bs * 512:(bs + 4) * 512], 1.0 / 64)
            dep_kr = [t_ksmm] + ([t_sbounce, t_esbounce] if m == 8 else [])
            t_krowA = p.op("vector", f_krowA, deps=dep_kr)
            t_kal = p.dma("kal", lambda s: s.dma_start(
                out=kstrow[0:2, 2 * T:3 * T], in_=ksq32[32:34, 0:T]), deps=[t_krowA])
            t_krB = p.op("vector", lambda ve, m=m: ve.tensor_tensor(
                kstrow[0:2, T:2 * T], kstrow[0:2, 0:T], kstrow[0:2, 0:T], ALU.mult),
                deps=[t_krowA])
            t_krow[m] = p.op("vector", lambda ve, m=m: ve.tensor_sub(
                kstrow[0:2, T:2 * T], kstrow[0:2, 2 * T:3 * T], kstrow[0:2, T:2 * T]),
                deps=[t_kal, t_krB])
            t_krs1 = p.op("scalar", lambda a, m=m: a.activation(
                out=kstrow[0:2, 3 * T:4 * T], in_=kstrow[0:2, T:2 * T], func=AF.Ln,
                bias=c_eps[0:2, 0:1]), deps=[t_krow[m]])
            t_krstd = p.op("scalar", lambda a, m=m: a.activation(
                out=kstrow[0:2, 3 * T:4 * T], in_=kstrow[0:2, 3 * T:4 * T],
                func=AF.Exp, scale=c_mhalf[0:2, 0:1]), deps=[t_krs1])
            t_kfold = p.op("vector", lambda ve, kr=kr: ve.tensor_single_scalar(
                kstrow[0:2, 3 * T:4 * T], kstrow[0:2, 3 * T:4 * T],
                gmbgT[0:2, kr:kr + 1], ALU.mult), deps=[t_krstd, t_cols])
            ro = 20 + 4 * (kr % 2)
            dep = [t_krow[m]] + ([t_kmu[m - 2], t_krs[m - 2]] if kr >= 2 else [])
            t_kb[m] = p.dma("rows4", lambda s, ro=ro: s.dma_start(
                out=d["rows_hbm"][ro:ro + 2, :], in_=kstrow[0:2, 0:T]), deps=dep)
            t_kb2 = p.dma("rows4", lambda s, ro=ro: s.dma_start(
                out=d["rows_hbm"][ro + 2:ro + 4, :], in_=kstrow[0:2, 3 * T:4 * T]),
                deps=[t_kfold])
            dep = [t_kb[m], t_kb2] + ([t_ktil[m - 2]] if kr >= 2 else [])
            t_kmu[m] = p.dma("kmu", lambda s, ro=ro: [
                s.dma_start(out=kmu_b[0:64, :],
                            in_=d["rows_hbm"][ro:ro + 1, :].broadcast_to((64, T))),
                s.dma_start(out=kmu_b[64:128, :],
                            in_=d["rows_hbm"][ro + 1:ro + 2, :].broadcast_to((64, T)))],
                deps=dep, n=2)
            t_krs[m] = p.dma("krs", lambda s, ro=ro: [
                s.dma_start(out=krs_b[0:64, :],
                            in_=d["rows_hbm"][ro + 2:ro + 3, :].broadcast_to((64, T))),
                s.dma_start(out=krs_b[64:128, :],
                            in_=d["rows_hbm"][ro + 3:ro + 4, :].broadcast_to((64, T)))],
                deps=dep, n=2)
            dep = [t_kmu[m], t_krs[m]] + ([t_spill[m - 2]] if m >= 2 else [])
            t_kt1 = p.op("vector", lambda ve, m=m: ve.tensor_tensor(
                kscr, evb[m % 2], kmu_b, ALU.subtract), deps=dep)
            t_ktil[m] = p.op("vector", lambda ve, m=m: ve.tensor_tensor(
                evb[m % 2], kscr, krs_b, ALU.mult), deps=[t_kt1])
            t_spill.append(p.dma(f"spl{m % 2}", lambda s, m=m, kr=kr: s.dma_start(
                out=d["kt_hbm"][kr * 128:(kr + 1) * 128, :], in_=evb[m % 2]), deps=[t_ktil[m]]))

    sp_r = [carve(AR1, i * 8192, T, F32) for i in range(4)]
    r_r = [carve(AR1, 32768 + i * 8192, T, F32) for i in range(4)]
    vb_r = [carve(AR2, 53248 + i * 4096, T, BF16) for i in range(2)]
    A2 = AR2
    sg_r = [carve(A2, i * 4096, T, BF16) for i in range(2)]
    th_r = [carve(A2, 8192 + i * 4096, T, BF16) for i in range(2)]
    g1_r = [carve(A2, 16384 + i * 4096, T, BF16) for i in range(2)]
    g2_r = [carve(A2, 24576 + i * 4096, T, BF16) for i in range(2)]
    g3_r = carve(A2, 32768, T, BF16)
    m_r = carve(A2, 36864, T, BF16)
    f_r = carve(A2, 40960, T, BF16)
    g_r = [carve(A2, 45056 + i * 4096, T, BF16) for i in range(2)]
    qh_b = [carve(AR3, 0, T, BF16), carve(AR3, 4096, T, BF16)]
    qb_b = [carve(AR3, 8192, T, BF16), carve(AR3, 12288, T, BF16)]
    kb_b = [carve(AR3, 16384, T, BF16), carve(AR3, 20480, T, BF16)]
    sbc_b = [carve(AR3, 24576, T, F32), carve(AR3, 32768, T, F32)]
    esbc_b = [carve(AR3, 40960, T, F32), carve(AR3, 49152, T, F32)]
    yev_b = [carve(AR3, 57344, T, BF16), carve(AR3, 61440, T, BF16)]

    NH = H
    tok_sp = {}; tok_r = {}; tok_sg = {}; tok_th = {}
    tok_g1 = {}; tok_g2 = {}; tok_g3 = {}; tok_m = {}; tok_sc = {}; tok_d5 = {}
    tok_vb = {}; tok_teb = {}; tok_teh = {}; tok_ev = {}; tok_yo = {}
    tok_qh = {}; tok_qb = {}; tok_kb = {}; tok_sbc = {}; tok_esbc = {}
    last_spill = t_spill[-1]
    all_spills = list(t_spill)

    for h in range(NH):
        hs = h % 2
        dep_q = all_spills if h < 2 else [tok_teh[h - 2]]
        tok_qh[h] = p.dma(f"qh{hs}", lambda s, h=h, hs=hs: s.dma_start(
            out=qh_b[hs][0:64, :], in_=d["q_hbm"][h * 64:(h + 1) * 64, :]), deps=dep_q)
        dep_qb = all_spills if h < 2 else [tok_d5[(h - 2, NBLK - 1)]]
        tok_qb[h] = p.dma(f"qb{hs}", lambda s, h=h, hs=hs: [
            s.dma_start(out=qb_b[hs][0:64, :], in_=d["q_hbm"][h * 64:(h + 1) * 64, :]),
            s.dma_start(out=qb_b[hs][64:128, :], in_=d["q_hbm"][h * 64:(h + 1) * 64, :])],
            deps=dep_qb, n=2)
        dep_kb = all_spills if h < 2 else [tok_g2[(h - 2, NBLK - 1)]]
        tok_kb[h] = p.dma(f"kb{hs}", lambda s, h=h, hs=hs: [
            s.dma_start(out=kb_b[hs][0:64, :], in_=d["kt_hbm"][h * 64:(h + 1) * 64, :]),
            s.dma_start(out=kb_b[hs][64:128, :], in_=d["kt_hbm"][h * 64:(h + 1) * 64, :])],
            deps=dep_kb, n=2)
        dep_sb = [t_sbounce, t_esbounce] if h < 2 else [tok_sg[(h - 2, NBLK - 1)]]
        tok_sbc[h] = p.dma(f"sbc{hs}", lambda s, h=h, hs=hs: s.dma_start(
            out=sbc_b[hs], in_=d["rows_hbm"][32 + h:33 + h, :].broadcast_to((128, T))), deps=dep_sb)
        dep_es = [t_sbounce, t_esbounce] if h < 2 else [tok_sp[(h - 2, NBLK - 1)]]
        tok_esbc[h] = p.dma(f"esbc{hs}", lambda s, h=h, hs=hs: s.dma_start(
            out=esbc_b[hs], in_=d["rows_hbm"][48 + h:49 + h, :].broadcast_to((128, T))), deps=dep_es)

        for g in range(NBLK // 4):
            for bb in range(4):
                b_ = g * 4 + bb
                bi = h * NBLK + b_
                ss = bi % 4
                col = h * 32 + b_
                dep = [tok_esbc[h], t_ew, t_cols]
                if bi >= 4:
                    pbk = ((bi - 4) // NBLK, (bi - 4) % NBLK)
                    dep += [(lambda k=pbk: tok_g1[k]), (lambda k=pbk: tok_th[k])]
                tok_sp[(h, b_)] = p.op("scalar", lambda a, hs=hs, ss=ss, col=col: a.activation(
                    out=sp_r[ss], in_=esbc_b[hs], func=AF.Ln,
                    scale=ewcols[:, col:col + 1], bias=1.0), deps=dep)
                dep = []
                if bi >= 4:
                    dep = [(lambda k=((bi - 4) // NBLK, (bi - 4) % NBLK): tok_sc[k])]
                dep = dep + [tok_sp[(h, b_)]]
                tok_r[(h, b_)] = p.op("scalar", lambda a, ss=ss, h=h, rs=bi % 4: a.activation(
                    out=r_r[rs], in_=sp_r[ss], func=AF.Exp, scale=pcolB[:, h:h + 1]), deps=dep)
            for bb in range(4):
                b_ = g * 4 + bb
                bi = h * NBLK + b_
                ss = bi % 4
                col = h * 32 + b_
                dep = [tok_sbc[h]]
                if bi >= 2:
                    dep += [(lambda k=((bi - 2) // NBLK, (bi - 2) % NBLK): tok_m[k])]
                tok_sg[(h, b_)] = p.op("scalar", lambda a, hs=hs, col=col, sl=bi % 2: a.activation(
                    out=sg_r[sl], in_=sbc_b[hs], func=AF.Sigmoid,
                    bias=wcols_s[:, col:col + 1]), deps=dep)
                dep = [t_cols]
                if bi >= 2:
                    dep += [(lambda k=((bi - 2) // NBLK, (bi - 2) % NBLK): tok_g1[k])]
                dep = dep + [tok_sp[(h, b_)]]
                tok_th[(h, b_)] = p.op("scalar", lambda a, ss=ss, h=h, sl=bi % 2: a.activation(
                    out=th_r[sl], in_=sp_r[ss], func=AF.Tanh,
                    scale=pcolA[:, h:h + 1], bias=pcolM[:, 0:1]), deps=dep)

        for b_ in range(NBLK):
            bi = h * NBLK + b_
            dep = list(all_spills) if h < 2 else []
            if bi >= 2:
                dep = dep + [(lambda k=((bi - 2) // NBLK, (bi - 2) % NBLK): tok_g2[k])]
            tok_vb[(h, b_)] = p.dma(f"vb{bi % 2}", lambda s, h=h, b_=b_, sl=bi % 2: [
                s.dma_start(out=vb_r[sl][0:64, :],
                            in_=d["v_hbm"][h * 64 + 2 * b_:h * 64 + 2 * b_ + 1, :]
                            .broadcast_to((64, T))),
                s.dma_start(out=vb_r[sl][64:128, :],
                            in_=d["v_hbm"][h * 64 + 2 * b_ + 1:h * 64 + 2 * b_ + 2, :]
                            .broadcast_to((64, T)))], deps=dep, n=2)

        for b_ in range(NBLK):
            bi = h * NBLK + b_
            dep = [tok_sp[(h, b_)], tok_th[(h, b_)]]
            if bi >= 2:
                dep += [(lambda k=((bi - 2) // NBLK, (bi - 2) % NBLK): tok_g3[k])]
            tok_g1[(h, b_)] = p.op("gpsimd", lambda ge, sl=bi % 2, ss=bi % 4: ge.tensor_tensor(
                g1_r[sl], sp_r[ss], th_r[sl], ALU.mult), deps=dep)
            dep = [tok_vb[(h, b_)], tok_kb[h]]
            if bi >= 2:
                dep += [(lambda k=((bi - 2) // NBLK, (bi - 2) % NBLK): tok_g3[k])]
            tok_g2[(h, b_)] = p.op("gpsimd", lambda ge, sl=bi % 2, hs=hs: ge.tensor_tensor(
                g2_r[sl], vb_r[sl], kb_b[hs], ALU.mult), deps=dep)

        for b_ in range(NBLK):
            bi = h * NBLK + b_
            dep = [tok_g1[(h, b_)], tok_g2[(h, b_)]]
            if bi >= 1:
                dep += [(lambda k=((bi - 1) // NBLK, (bi - 1) % NBLK): tok_m[k]),
                        (lambda k=((bi - 1) // NBLK, (bi - 1) % NBLK): tok_sc[k]),
                        (lambda k=((bi - 1) // NBLK, (bi - 1) % NBLK): tok_d5[k])]
            tok_g3[(h, b_)] = p.op("vector", lambda ve, sl=bi % 2: ve.tensor_tensor(
                g3_r, g1_r[sl], g2_r[sl], ALU.mult), deps=dep)
            tok_m[(h, b_)] = p.op("vector", lambda ve, sl=bi % 2: ve.tensor_tensor(
                m_r, sg_r[sl], g3_r, ALU.mult),
                deps=[tok_sg[(h, b_)], tok_g3[(h, b_)]])
            tok_sc[(h, b_)] = p.op("vector", lambda ve, rs=bi % 4: ve.tensor_tensor_scan(
                out=f_r, data0=r_r[rs], data1=m_r, initial=0.0,
                op0=ALU.mult, op1=ALU.add), deps=[tok_r[(h, b_)], tok_m[(h, b_)]])
            dep = [tok_qb[h], tok_sc[(h, b_)]]
            if bi >= 2:
                dep += [(lambda k=((bi - 2) // NBLK, (bi - 2) % NBLK): tok_teb[k])]
            tok_d5[(h, b_)] = p.op("vector", lambda ve, sl=bi % 2, hs=hs: ve.tensor_tensor(
                g_r[sl], f_r, qb_b[hs], ALU.mult), deps=dep)

        ypq = PS[0:64, 0:2048] if hs == 0 else PS[0:64, 2048:4096]
        def f_wq(te, h=h, hs=hs, ypq=ypq):
            last = None
            for n in range(4):
                last = te.matmul(ypq[:, n * 512:(n + 1) * 512],
                                 wjib[0:64, h * 64:(h + 1) * 64],
                                 qh_b[hs][0:64, n * 512:(n + 1) * 512],
                                 start=True, stop=False)
            return last
        dep = [tok_qh[h], t_par]
        if h == 0:
            dep += [t_evq[22], t_krow[14]]
        elif h == 1:
            dep += [t_evq[23], t_krow[15]]
        else:
            dep += [tok_ev[h - 2]]
        tok_wq = p.op("tensor", f_wq, deps=dep)
        for b_ in range(NBLK):
            bi = h * NBLK + b_
            def f_sel(te, b_=b_, sl=bi % 2, ypq=ypq):
                last = None
                for n in range(4):
                    last = te.matmul(ypq[:, n * 512:(n + 1) * 512],
                                     sel_s[:, b_ * 64:(b_ + 1) * 64],
                                     g_r[sl][:, n * 512:(n + 1) * 512],
                                     start=False, stop=(b_ == NBLK - 1))
                return last
            tok_teb[(h, b_)] = p.op("tensor", f_sel, deps=[tok_d5[(h, b_)], t_small[-1]])
        tok_teh[h] = tok_teb[(h, NBLK - 1)]

        dep = [tok_teh[h]]
        if h >= 2:
            dep += [tok_yo[h - 2]]
        tok_ev[h] = p.op("vector", lambda ve, hs=hs, ypq=ypq: ve.tensor_copy(
            yev_b[hs][0:64, :], ypq), deps=dep)
        tok_yo[h] = p.dma(f"yo{hs}", lambda s, h=h, hs=hs: s.dma_start(
            out=d["y_hbm"][h * 64:(h + 1) * 64, :], in_=yev_b[hs][0:64, :]), deps=[tok_ev[h]])

    yv = [carve(AR2, k * 4096, T, BF16) for k in range(KT)]
    t_yld = []
    for k in range(KT):
        t_yld.append(p.dma("yin", lambda s, k=k: s.dma_start(
            out=yv[k], in_=d["y_hbm"][k * 128:(k + 1) * 128, :]),
            deps=[tok_yo[NH - 1], tok_yo[NH - 2]]))
    xs2 = [carve(AR3, 8192 * k, T, F32) for k in range(KT)]
    dep_xrl = [tok_sg[(NH - 1, NBLK - 1)], tok_sp[(NH - 1, NBLK - 1)],
               tok_d5[(NH - 1, NBLK - 1)], tok_g2[(NH - 1, NBLK - 1)],
               tok_yo[NH - 1], tok_yo[NH - 2], tok_teh[NH - 1]]
    t_xrl = []
    for k in range(KT):
        t_xrl.append(p.dma("xrl", lambda s, k=k: s.dma_start(
            out=xs2[k], in_=d["x"][k * 128:(k + 1) * 128, :]), deps=dep_xrl))
    wo_stage = [carve(AR1, 0, 1024, F32), carve(AR1, 4096, 1024, F32)]
    wo_b = [carve(AR1, 8192, 1024, BF16), carve(AR1, 10240, 1024, BF16)]
    x1st = [carve(AR1, 12288, T, F32), carve(AR1, 20480, T, F32)]
    t_wold, t_word, t_womm, t_evo, t_xo = [], [], [], [], []
    for m in range(KT):
        bs = 4 * (m % 2)
        dep = [tok_sc[(NH - 1, NBLK - 1)], tok_r[(NH - 1, NBLK - 1)]] if m < 2 else [t_word[m - 2]]
        t_wold.append(p.dma(f"wo{m % 2}", lambda s, m=m: s.dma_start(
            out=wo_stage[m % 2].rearrange("p (kt c) -> p kt c", kt=KT),
            in_=d["woT"].rearrange("(kt p) c -> p kt c", p=128)[:, :, m * 128:(m + 1) * 128]),
            deps=dep))
        dep = [t_wold[m]] + ([t_womm[m - 2]] if m >= 2 else [])
        t_word.append(p.op("vector", lambda ve, m=m: ve.tensor_copy(wo_b[m % 2], wo_stage[m % 2]),
                           deps=dep))
        def f_womm(te, m=m, bs=bs):
            last = None
            for n in range(4):
                for k in range(KT):
                    last = te.matmul(bank(bs + n), wo_b[m % 2][:, k * 128:(k + 1) * 128],
                                     yv[k][:, n * 512:(n + 1) * 512],
                                     start=(k == 0), stop=(k == KT - 1))
            return last
        dep = [t_word[m], t_yld[-1]] + ([t_evo[m - 2]] if m >= 2 else [])
        if m < 2:
            dep += [tok_ev[NH - 1], tok_ev[NH - 2]]
        t_womm.append(p.op("tensor", f_womm, deps=dep))
        def f_evo(ve, m=m, bs=bs):
            last = None
            for n in range(4):
                last = ve.tensor_tensor(x1st[m % 2][:, n * 512:(n + 1) * 512], bank(bs + n),
                                        xs2[m][:, n * 512:(n + 1) * 512], ALU.add)
            return last
        dep = [t_womm[m], t_xrl[-1]] + ([t_xo[m - 2]] if m >= 2 else [])
        t_evo.append(p.op("vector", f_evo, deps=dep))
        t_xo.append(p.dma(f"xo{m % 2}", lambda s, m=m: s.dma_start(
            out=d["x1"][m * 128:(m + 1) * 128, :], in_=x1st[m % 2]), deps=[t_evo[m]]))

    p.op("gpsimd", lambda ge: None, deps=t_xo)
    p.emit()
    return nc




def build_ffn():
    nc = bass.Bass("TRN2", target_bir_lowering=False, debug=False)
    d = {}
    names = ["x", "wfT", "wiT", "wcT", "ufT", "uiT", "ucT", "woT", "uoT"]
    shapes = [[D, T]] + [[D, D]] * 8
    for nm, shp in zip(names, shapes):
        d[nm] = nc.dram_tensor(nm, shp, F32, kind="ExternalInput").ap()
    d["bcol"] = nc.dram_tensor("bcol", [128, 32], F32, kind="ExternalInput").ap()
    d["incol"] = nc.dram_tensor("incol", [128, 8], F32, kind="ExternalInput").ap()
    d["x2"] = nc.dram_tensor("x2", [D, T], F32, kind="ExternalOutput").ap()
    d["rows_hbm"] = nc.dram_tensor("rows_hbm", [4, T], F32).ap()
    emit_ffn(nc, d)
    return nc


def emit_ffn(nc, d):
    CH = 256
    p = Prog(nc)
    AR1 = nc.alloc_sbuf_tensor("AR1", [128, 16384], F32).ap()
    AR2 = nc.alloc_sbuf_tensor("AR2", [128, 16384], F32).ap()
    AR3 = nc.alloc_sbuf_tensor("AR3", [128, 16384], F32).ap()
    SM = nc.alloc_sbuf_tensor("SM", [128, 1200], F32).ap()
    PS = nc.alloc_psum_tensor("PS", [128, 4096], F32).ap()

    bcol = carve(SM, 0, 32, F32)
    incol = carve(SM, 128, 8, F32)
    onescol = carve(SM, 160, 1, F32)
    onescol_b = carve(SM, 164, 1, BF16)
    c_eps = carve(SM, 168, 1, F32)
    c_mhalf = carve(SM, 172, 1, F32)

    xs = AR1
    t_x = p.dma("xin", lambda s: s.dma_start(
        out=xs.rearrange("p (kt t) -> p kt t", kt=KT),
        in_=d["x"].rearrange("(kt p) t -> p kt t", p=128)))
    t_b = p.dma("sml", lambda s: s.dma_start(out=bcol, in_=d["bcol"]))
    t_ic = p.dma("sml", lambda s: s.dma_start(out=incol, in_=d["incol"]))
    t_sml = t_ic
    def f_consts(ve):
        ve.memset(onescol[:, 0:1], 1.0)
        ve.memset(c_eps[:, 0:1], 1e-5)
        ve.memset(c_mhalf[:, 0:1], -0.5)
        return None
    t_c0 = p.op("vector", f_consts)
    t_par = p.op("vector", lambda ve: ve.tensor_copy(onescol_b[:, 0:1], onescol[:, 0:1]),
                 deps=[t_c0])

    xr = AR2.bitcast(BF16)[:, 0:KT * T]
    sqring = [carve(AR3, 0, T, BF16), carve(AR3, 8192, T, BF16)]
    strow = carve(AR3, 16384, 4 * T, F32)
    mu_b = carve(AR3, 32768, T, F32)
    rstd_b = carve(AR3, 40960, T, F32)
    nrm_scr = carve(AR3, 49152, T, F32)
    t_xr, t_sq, t_mmx, t_mmq = [], [], [], []
    for k in range(KT):
        t_xr.append(p.op("vector", lambda ve, k=k: ve.tensor_copy(
            xr[:, k * T:(k + 1) * T], xs[:, k * T:(k + 1) * T]), deps=[t_x]))
        dep = [t_x] + ([t_mmq[k - 2]] if k >= 2 else [])
        t_sq.append(p.op("scalar", lambda a, k=k: a.activation(
            out=sqring[k % 2], in_=xs[:, k * T:(k + 1) * T], func=AF.Square), deps=dep))
        def f_sx(te, k=k):
            last = None
            for n in range(4):
                last = te.matmul(PS[0:1, n * 512:(n + 1) * 512], onescol_b[:, 0:1],
                                 xr[:, k * T + n * 512:k * T + (n + 1) * 512],
                                 start=(k == 0), stop=(k == KT - 1))
            return last
        t_mmx.append(p.op("tensor", f_sx, deps=[t_xr[k], t_par]))
        def f_sq2(te, k=k):
            last = None
            for n in range(4):
                last = te.matmul(PS[0:1, (4 + n) * 512:(5 + n) * 512], onescol_b[:, 0:1],
                                 sqring[k % 2][:, n * 512:(n + 1) * 512],
                                 start=(k == 0), stop=(k == KT - 1))
            return last
        t_mmq.append(p.op("tensor", f_sq2, deps=[t_sq[k], t_par]))
    MU, VA, EX, RS = 0, T, 2 * T, 3 * T
    t_st1 = p.op("vector", lambda ve: ve.tensor_scalar_mul(
        strow[0:1, MU:MU + T], PS[0:1, 0:2048], 1.0 / D), deps=[t_mmx[-1], t_mmq[-1]])
    t_st2 = p.op("vector", lambda ve: ve.tensor_scalar_mul(
        strow[0:1, EX:EX + T], PS[0:1, 2048:4096], 1.0 / D), deps=[t_mmx[-1], t_mmq[-1]])
    t_st3 = p.op("vector", lambda ve: ve.tensor_tensor(
        strow[0:1, VA:VA + T], strow[0:1, MU:MU + T], strow[0:1, MU:MU + T], ALU.mult),
        deps=[t_st1])
    t_stats = p.op("vector", lambda ve: ve.tensor_sub(
        strow[0:1, VA:VA + T], strow[0:1, EX:EX + T], strow[0:1, VA:VA + T]),
        deps=[t_st2, t_st3])
    t_rs1 = p.op("scalar", lambda a: a.activation(
        out=strow[0:1, RS:RS + T], in_=strow[0:1, VA:VA + T], func=AF.Ln,
        bias=c_eps[0:1, 0:1]), deps=[t_stats])
    t_rstd = p.op("scalar", lambda a: a.activation(
        out=strow[0:1, RS:RS + T], in_=strow[0:1, RS:RS + T], func=AF.Exp,
        scale=c_mhalf[0:1, 0:1]), deps=[t_rs1])
    t_rw1 = p.dma("rows", lambda s: s.dma_start(out=d["rows_hbm"][0:1, :],
                                                in_=strow[0:1, MU:MU + T]), deps=[t_stats])
    t_rw2 = p.dma("rows", lambda s: s.dma_start(out=d["rows_hbm"][1:2, :],
                                                in_=strow[0:1, RS:RS + T]), deps=[t_rstd])
    t_mub = p.dma("mub", lambda s: s.dma_start(
        out=mu_b, in_=d["rows_hbm"][0:1, :].broadcast_to((128, T))), deps=[t_rw1, t_rw2])
    t_rsb = p.dma("rsb", lambda s: s.dma_start(
        out=rstd_b, in_=d["rows_hbm"][1:2, :].broadcast_to((128, T))), deps=[t_rw1, t_rw2])
    xn = AR2.bitcast(BF16)[:, 0:KT * T]
    t_xn = []
    for k in range(KT):
        dep = [t_mub, t_rsb, t_mmq[k], t_mmx[k]]
        if t_xn:
            dep.append(t_xn[-1])
        t_na = p.op("vector", lambda ve, k=k: ve.tensor_tensor(
            nrm_scr, xs[:, k * T:(k + 1) * T], mu_b, ALU.subtract), deps=dep)
        t_xn.append(p.op("vector", lambda ve, k=k: ve.tensor_tensor(
            xn[:, k * T:(k + 1) * T], nrm_scr, rstd_b, ALU.mult), deps=[t_na]))

    c_t = carve(AR2, 32768, T, F32)
    c_b = carve(SM, 176, T, BF16)
    x2st = carve(AR2, 40960, T, F32)
    wstage = [carve(AR2, 49152, 1024, F32), carve(AR2, 53248, 1024, F32)]
    uostage = [carve(AR2, 57344, 1024, F32), carve(AR2, 61440, 1024, F32)]
    wres = {"wfT": carve(AR1, 0, 8192, BF16), "wiT": carve(AR1, 16384, 8192, BF16),
            "wcT": carve(AR1, 32768, 8192, BF16), "ufT": carve(AR3, 0, 8192, BF16),
            "uiT": carve(AR3, 16384, 8192, BF16), "ucT": carve(AR3, 32768, 8192, BF16)}
    g_f = carve(AR1, 49152, T, F32)
    g_g = carve(AR1, 57344, T, F32)
    g_i = carve(AR3, 49152, T, F32)
    x1rl = carve(AR3, 57344, T, F32)

    t_ci = []
    for k in range(KT):
        tz = p.op("vector", lambda ve, k=k: ve.memset(c_t[:, k * CH:(k + 1) * CH], 0.0),
                  deps=[t_sml] + ([t_ci[-1]] if t_ci else []))
        t_ci.append(p.op("vector", lambda ve, k=k: ve.tensor_scalar_add(
            c_t[:, k * CH:(k + 1) * CH], c_t[:, k * CH:(k + 1) * CH],
            incol[:, k:k + 1]), deps=[tz]))
    t_ci.append(p.op("vector", lambda ve: ve.tensor_copy(c_b, c_t), deps=[t_ci[-1]]))

    wnames = ["wfT", "wiT", "wcT", "ufT", "uiT", "ucT"]
    t_wld, t_wcv = {}, {}
    for wi_, wn in enumerate(wnames):
        for mch in range(KT):
            i = wi_ * KT + mch
            if i < 2:
                dep = [t_xn[-1], t_sq[-1]]
            else:
                pw = wnames[(i - 2) // KT]
                dep = [t_wcv[(pw, (i - 2) % KT)]]
            t_wld[(wn, mch)] = p.dma(f"wld{i % 2}", lambda s, wn=wn, mch=mch, i=i: s.dma_start(
                out=wstage[i % 2].rearrange("p (kt c) -> p kt c", kt=KT),
                in_=d[wn].rearrange("(kt p) c -> p kt c", p=128)[:, :, mch * 128:(mch + 1) * 128]),
                deps=dep)
            def f_cv(ve, wn=wn, mch=mch, i=i):
                return ve.tensor_copy(
                    wres[wn].rearrange("p (kt m c) -> p kt m c", kt=KT, m=KT)[:, :, mch, :],
                    wstage[i % 2].rearrange("p (kt c) -> p kt c", kt=KT))
            dep = [t_wld[(wn, mch)]]
            if wn.startswith("u") or wi_ >= 1:
                dep += [t_xn[-1]]
            t_wcv[(wn, mch)] = p.op("vector", f_cv, deps=dep)
    t_wall = p.op("vector", lambda ve: None,
                  deps=[t_wcv[(wn, m)] for wn in wnames for m in range(KT)])

    def wsl(wn, k, m):
        return wres[wn][:, (k * KT + m) * 128:(k * KT + m) * 128 + 128]

    GATES = [("wfT", "ufT", 0, AF.Sigmoid, g_f),
             ("wiT", "uiT", 1, AF.Sigmoid, g_i),
             ("wcT", "ucT", 2, AF.Tanh, g_g)]
    tok_gate = {}
    tok_cup = {}
    for tau in range(8):
        for gi_, (wn, un, bg, fn, gdst) in enumerate(GATES):
            grp = (tau * 3 + gi_) % 2
            def f_gmm(te, tau=tau, wn=wn, un=un, grp=grp):
                last = None
                for m in range(KT):
                    pb = PS[:, grp * 2048 + m * 256: grp * 2048 + (m + 1) * 256]
                    for k in range(KT):
                        last = te.matmul(pb, wsl(wn, k, m),
                                         xn[:, k * T + tau * CH:k * T + (tau + 1) * CH],
                                         start=(k == 0), stop=False)
                    for k in range(KT):
                        last = te.matmul(pb, wsl(un, k, m),
                                         c_b[:, k * CH:(k + 1) * CH],
                                         start=False, stop=(k == KT - 1))
                return last
            dep = [t_wall, t_ci[-1]]
            if tau > 0:
                dep += [tok_cup[(tau - 1, 2)]]
            if tau * 3 + gi_ >= 2:
                pg = tau * 3 + gi_ - 2
                dep += [tok_gate[(pg // 3, pg % 3)]]
            t_gmm = p.op("tensor", f_gmm, deps=dep)
            def f_gev(a, bg=bg, fn=fn, gdst=gdst, grp=grp):
                last = None
                for m in range(KT):
                    last = a.activation(
                        out=gdst[:, m * CH:(m + 1) * CH],
                        in_=PS[:, grp * 2048 + m * 256: grp * 2048 + (m + 1) * 256],
                        func=fn, bias=bcol[:, bg * 8 + m:bg * 8 + m + 1])
                return last
            dep = [t_gmm, t_sml]
            if tau > 0:
                dep += [tok_cup[(tau - 1, gi_)]]
            tok_gate[(tau, gi_)] = p.op("scalar", f_gev, deps=dep)
        t_u1 = p.op("vector", lambda ve, tau=tau: ve.tensor_tensor(
            x2st, g_i, g_g, ALU.mult),
            deps=[tok_gate[(tau, 1)], tok_gate[(tau, 2)]]
                 + ([tok_cup[(tau - 1, 2)]] if tau > 0 else []))
        t_u2 = p.op("vector", lambda ve, tau=tau: ve.tensor_tensor(
            c_t, g_f, c_t, ALU.mult), deps=[tok_gate[(tau, 0)], t_u1])
        tk = p.op("vector", lambda ve, tau=tau: ve.tensor_tensor(
            c_t, c_t, x2st, ALU.add), deps=[t_u2])
        tk = p.op("vector", lambda ve: ve.tensor_copy(c_b, c_t), deps=[tk])
        tok_cup[(tau, 0)] = tok_cup[(tau, 1)] = tok_cup[(tau, 2)] = tk

    wobuf = [carve(AR1, 49152, 1024, BF16), carve(AR1, 51200, 1024, BF16)]
    uobuf = [carve(AR1, 53248, 1024, BF16), carve(AR1, 55296, 1024, BF16)]
    t_wocv = {}
    t_ho = {}
    t_xo = []
    t_x1m = {}
    for m in range(KT):
        i = m % 2
        dep = [tok_cup[(7, 2)]] if m < 2 else [t_wocv[m - 2]]
        t_wo1 = p.dma(f"wo{i}", lambda s, m=m, i=i: s.dma_start(
            out=wstage[i].rearrange("p (kt c) -> p kt c", kt=KT),
            in_=d["woT"].rearrange("(kt p) c -> p kt c", p=128)[:, :, m * 128:(m + 1) * 128]),
            deps=dep)
        t_uo1 = p.dma(f"uo{i}", lambda s, m=m, i=i: s.dma_start(
            out=uostage[i].rearrange("p (kt c) -> p kt c", kt=KT),
            in_=d["uoT"].rearrange("(kt p) c -> p kt c", p=128)[:, :, m * 128:(m + 1) * 128]),
            deps=dep)
        def f_wocv(ve, m=m, i=i):
            ve.tensor_copy(wobuf[i], wstage[i])
            return ve.tensor_copy(uobuf[i], uostage[i])
        dep2 = [t_wo1, t_uo1, tok_cup[(7, 2)]]
        if m >= 2:
            dep2 += [t_ho[m - 2]]
        t_wocv[m] = p.op("vector", f_wocv, deps=dep2)
        t_x1m[m] = p.dma(f"x1l{i}", lambda s, m=m: s.dma_start(
            out=x1rl, in_=d["x"][m * 128:(m + 1) * 128, :]),
            deps=[tok_cup[(7, 2)], t_xn[-1]] if m < 1 else [t_ho[m - 1]])
        def f_hmm(te, m=m, i=i):
            last = None
            for tau in range(8):
                pb = PS[:, (m % 2) * 2048 + tau * 256:(m % 2) * 2048 + (tau + 1) * 256]
                for k in range(KT):
                    last = te.matmul(pb, wobuf[i][:, k * 128:(k + 1) * 128],
                                     xn[:, k * T + tau * CH:k * T + (tau + 1) * CH],
                                     start=(k == 0), stop=False)
                for k in range(KT):
                    last = te.matmul(pb, uobuf[i][:, k * 128:(k + 1) * 128],
                                     c_b[:, k * CH:(k + 1) * CH],
                                     start=False, stop=(k == KT - 1))
            return last
        dep = [t_wocv[m]]
        if m >= 2:
            dep += [t_ho[m - 2]]
        else:
            dep += [tok_gate[(7, 2)], tok_gate[(7, 1)]]
        t_hmm = p.op("tensor", f_hmm, deps=dep)
        def f_hev(a, m=m):
            last = None
            for tau in range(8):
                last = a.activation(
                    out=x2st[:, tau * CH:(tau + 1) * CH],
                    in_=PS[:, (m % 2) * 2048 + tau * 256:(m % 2) * 2048 + (tau + 1) * 256],
                    func=AF.Tanh, bias=bcol[:, 24 + m:25 + m])
            return last
        dep = [t_hmm, tok_cup[(7, 2)]] + ([t_xo[m - 1]] if m >= 1 else [])
        t_hev = p.op("scalar", f_hev, deps=dep)
        t_ho[m] = p.op("vector", lambda ve, m=m: ve.tensor_tensor(
            x2st, x2st, x1rl, ALU.add), deps=[t_hev, t_x1m[m]])
        t_xo.append(p.dma(f"xo{m % 2}", lambda s, m=m: s.dma_start(
            out=d["x2"][m * 128:(m + 1) * 128, :], in_=x2st), deps=[t_ho[m]]))

    p.op("gpsimd", lambda ge: None, deps=t_xo)
    p.emit()
    return nc




def build_attn_b():
    nc = bass.Bass("TRN2", target_bir_lowering=False, debug=False)
    d = {}
    ins = [("x", [D, T]), ("muxcol", [128, 8]), ("aT", [D, 192]),
           ("abcol", [128, 2]), ("bT", [32, 6 * D]), ("lamcol", [128, 48]),
           ("b2col", [128, 48]), ("mhdT", [64, 5 * 16 * 64]), ("gwT", [D, D]),
           ("wkvaT", [64, 1024]), ("wkvbT", [64, 1024]), ("wlamcol", [128, 8]),
           ("gncol", [128, 8]), ("gnbcol", [128, 8]), ("owT", [D, D]),
           ("obcol", [128, 8])]
    for nm, shp in ins:
        d[nm] = nc.dram_tensor(nm, shp, F32, kind="ExternalInput").ap()
    d["x3"] = nc.dram_tensor("x3", [D, T], F32, kind="ExternalOutput").ap()
    for nm in ("g_hbm", "k_hbm", "v_hbm", "r_hbm", "u_hbm", "w_hbm", "qd_hbm",
               "kh_hbm", "rt_hbm", "wd_hbm", "dd5_sp"):
        d[nm] = nc.dram_tensor(nm, [D, T], BF16).ap()
    if DBG_Z:
        d["z_dbg"] = nc.dram_tensor("z_dbg", [D, T], BF16).ap()
    d["hdn_sp"] = nc.dram_tensor("hdn_sp", [192, T], BF16).ap()
    d["rows_hbm"] = nc.dram_tensor("rows_hbm", [40, T], F32).ap()
    hsel_np = np.zeros((128, KT, 16), dtype=ml_dtypes.bfloat16)
    dupsel_np = np.zeros((16, KT * 128), dtype=ml_dtypes.bfloat16)
    for kt in range(KT):
        for pp in range(128):
            h = 2 * kt + pp // 64
            hsel_np[pp, kt, h] = 1.0
            dupsel_np[h, kt * 128 + pp] = 1.0
    d["hsel"] = nc.inline_tensor(np.ascontiguousarray(hsel_np.reshape(128, KT * 16)), "hsel_b").ap()
    d["hself"] = nc.inline_tensor(
        np.ascontiguousarray(hsel_np.reshape(128, KT * 16).astype(np.float32)), "hsel_f").ap()
    d["dupsel"] = nc.inline_tensor(np.ascontiguousarray(dupsel_np), "dupsel_b").ap()
    d["ident"] = nc.inline_tensor(np.eye(128, dtype=np.float32), "ident_c").ap()
    emit_attn_b(nc, d)
    return nc


def emit_attn_b(nc, d):
    p = Prog(nc)
    AR1 = nc.alloc_sbuf_tensor("AR1", [128, 16384], F32).ap()
    AR2 = nc.alloc_sbuf_tensor("AR2", [128, 16384], F32).ap()
    AR3 = nc.alloc_sbuf_tensor("AR3", [128, 16384], F32).ap()
    SM = nc.alloc_sbuf_tensor("SM", [128, 880], F32).ap()
    PS = nc.alloc_psum_tensor("PS", [128, 4096], F32).ap()
    QT = T // 4

    off = [0]
    def sm(n, dtype=F32):
        v = carve(SM, off[0], n, dtype)
        off[0] += ((n * (2 if dtype == BF16 else 4) + 3) // 4) * 4
        return v
    muxcol = sm(8); abcol = sm(2); lamcol = sm(48); b2col = sm(48)
    wlamcol = sm(8); gncol = sm(8); gnbcol = sm(8); obcol = sm(8)
    hsel = sm(128, BF16); dupsel_s = sm(1024, BF16); hself = sm(128)
    onescol = sm(4); onescol_b = sm(4, BF16)
    c_eps = sm(4); c_gneps = sm(4); c_mhalf = sm(4)
    assert off[0] <= 880 * 4, off[0]

    xs = AR1
    t_x = p.dma("xin", lambda s: s.dma_start(
        out=xs.rearrange("p (kt t) -> p kt t", kt=KT),
        in_=d["x"].rearrange("(kt p) t -> p kt t", p=128)))
    t_small = []
    for nm, dst in [("muxcol", muxcol), ("abcol", abcol), ("lamcol", lamcol),
                    ("b2col", b2col), ("wlamcol", wlamcol), ("gncol", gncol),
                    ("gnbcol", gnbcol), ("obcol", obcol)]:
        t_small.append(p.dma("sml", lambda s, nm=nm, dst=dst: s.dma_start(out=dst, in_=d[nm])))
    t_small.append(p.dma("sml", lambda s: s.dma_start(out=hsel, in_=d["hsel"])))
    t_small.append(p.dma("sml", lambda s: s.dma_start(out=dupsel_s[0:16, :], in_=d["dupsel"])))
    t_small.append(p.dma("sml", lambda s: s.dma_start(out=hself, in_=d["hself"])))
    t_sml = t_small[-1]
    def f_consts(ve):
        ve.memset(onescol[:, 0:1], 1.0)
        ve.memset(c_eps[:, 0:1], 1e-5)
        ve.memset(c_gneps[:, 0:1], 1e-3)
        ve.memset(c_mhalf[:, 0:1], -0.5)
        return None
    t_c0 = p.op("vector", f_consts)
    t_par = p.op("vector", lambda ve: ve.tensor_copy(onescol_b[:, 0:1], onescol[:, 0:1]),
                 deps=[t_c0])

    xr = AR3.bitcast(BF16)[:, 0:KT * T]
    sqring = [carve(AR3, 32768, T, BF16), carve(AR3, 36864, T, BF16)]
    strow = carve(AR3, 40960, 3 * T, F32)
    t_xr, t_sq, t_mmx, t_mmq = [], [], [], []
    for k in range(KT):
        t_xr.append(p.op("vector", lambda ve, k=k: ve.tensor_copy(
            xr[:, k * T:(k + 1) * T], xs[:, k * T:(k + 1) * T]), deps=[t_x]))
        dep = [t_x] + ([t_mmq[k - 2]] if k >= 2 else [])
        t_sq.append(p.op("scalar", lambda a, k=k: a.activation(
            out=sqring[k % 2], in_=xs[:, k * T:(k + 1) * T], func=AF.Square), deps=dep))
        def f_sx(te, k=k):
            last = None
            for n in range(4):
                last = te.matmul(PS[0:1, n * 512:(n + 1) * 512], onescol_b[:, 0:1],
                                 xr[:, k * T + n * 512:k * T + (n + 1) * 512],
                                 start=(k == 0), stop=(k == KT - 1))
            return last
        t_mmx.append(p.op("tensor", f_sx, deps=[t_xr[k], t_par]))
        def f_sq2(te, k=k):
            last = None
            for n in range(4):
                last = te.matmul(PS[0:1, (4 + n) * 512:(5 + n) * 512], onescol_b[:, 0:1],
                                 sqring[k % 2][:, n * 512:(n + 1) * 512],
                                 start=(k == 0), stop=(k == KT - 1))
            return last
        t_mmq.append(p.op("tensor", f_sq2, deps=[t_sq[k], t_par]))
    MU, VA, EX, RS = 0, T, 2 * T, 2 * T
    t_st1 = p.op("vector", lambda ve: ve.tensor_scalar_mul(
        strow[0:1, MU:MU + T], PS[0:1, 0:2048], 1.0 / D), deps=[t_mmx[-1], t_mmq[-1]])
    t_st2 = p.op("vector", lambda ve: ve.tensor_scalar_mul(
        strow[0:1, EX:EX + T], PS[0:1, 2048:4096], 1.0 / D), deps=[t_mmx[-1], t_mmq[-1]])
    t_st3 = p.op("vector", lambda ve: ve.tensor_tensor(
        strow[0:1, VA:VA + T], strow[0:1, MU:MU + T], strow[0:1, MU:MU + T], ALU.mult),
        deps=[t_st1])
    t_stats = p.op("vector", lambda ve: ve.tensor_sub(
        strow[0:1, VA:VA + T], strow[0:1, EX:EX + T], strow[0:1, VA:VA + T]),
        deps=[t_st2, t_st3])
    t_rs1 = p.op("scalar", lambda a: a.activation(
        out=strow[0:1, RS:RS + T], in_=strow[0:1, VA:VA + T], func=AF.Ln,
        bias=c_eps[0:1, 0:1]), deps=[t_stats, t_c0])
    t_rstd = p.op("scalar", lambda a: a.activation(
        out=strow[0:1, RS:RS + T], in_=strow[0:1, RS:RS + T], func=AF.Exp,
        scale=c_mhalf[0:1, 0:1]), deps=[t_rs1])
    t_rw1 = p.dma("rows", lambda s: s.dma_start(out=d["rows_hbm"][0:1, :],
                                                in_=strow[0:1, MU:MU + T]), deps=[t_stats])
    t_rw2 = p.dma("rows", lambda s: s.dma_start(out=d["rows_hbm"][1:2, :],
                                                in_=strow[0:1, RS:RS + T]), deps=[t_rstd])
    mu_b = carve(AR3, 0, T, F32)
    rstd_b = carve(AR3, 8192, T, F32)
    t_mub = p.dma("mub", lambda s: s.dma_start(
        out=mu_b, in_=d["rows_hbm"][0:1, :].broadcast_to((128, T))),
        deps=[t_rw1, t_rw2, t_mmx[-1], t_mmq[-1]])
    t_rsb = p.dma("rsb", lambda s: s.dma_start(
        out=rstd_b, in_=d["rows_hbm"][1:2, :].broadcast_to((128, T))),
        deps=[t_rw1, t_rw2, t_mmx[-1], t_mmq[-1]])
    xnf = AR2
    nrm_scr = carve(AR3, 49152, T, F32)
    t_xn = []
    for k in range(KT):
        dep = [t_mub, t_rsb, t_mmq[k], t_mmx[k]]
        if t_xn:
            dep.append(t_xn[-1])
        t_na = p.op("vector", lambda ve, k=k: ve.tensor_tensor(
            nrm_scr, xs[:, k * T:(k + 1) * T], mu_b, ALU.subtract), deps=dep)
        t_xn.append(p.op("vector", lambda ve, k=k: ve.tensor_tensor(
            xnf[:, k * T:(k + 1) * T], nrm_scr, rstd_b, ALU.mult), deps=[t_na]))

    diff = AR1
    t_diff = []
    for k in range(KT):
        t_da = p.op("vector", lambda ve, k=k: ve.tensor_tensor(
            diff[:, k * T + 1:(k + 1) * T], xnf[:, k * T:(k + 1) * T - 1],
            xnf[:, k * T + 1:(k + 1) * T], ALU.subtract), deps=[t_xn[-1], t_sq[-1]])
        t_diff.append(p.op("vector", lambda ve, k=k: ve.tensor_scalar_mul(
            diff[:, k * T:k * T + 1], xnf[:, k * T:k * T + 1], -1.0), deps=[t_da]))

    z_bf = carve(AR3, 24576, T, BF16)
    zscr = carve(AR3, 28672, T, F32)
    astg = [carve(AR3, 49152, 192, F32), carve(AR3, 50176, 192, F32)]
    ab_s = [carve(AR3, 51200, 192, BF16), carve(AR3, 51584, 192, BF16)]
    t_ald, t_acv = [], []

    def emit_ald(k):
        dep = [t_xn[-1], t_rw1, t_rw2] if k < 2 else [t_acv[k - 2]]
        t_ald.append(p.dma(f"ald{k % 2}", lambda s, k=k: s.dma_start(
            out=astg[k % 2][0:128, :], in_=d["aT"][k * 128:(k + 1) * 128, :]), deps=dep))
        t_acv.append(p.op("vector", lambda ve, k=k: ve.tensor_copy(
            ab_s[k % 2][0:128, :], astg[k % 2][0:128, :]),
            deps=[t_ald[k]] + ([(lambda k=k: t_zmm[k - 2])] if k >= 2 else [])))
    emit_ald(0)
    emit_ald(1)
    t_zmm = []
    t_zdbg = []
    for k in range(KT):
        dep = [t_diff[k], t_sml] + ([t_zmm[-1]] if t_zmm else [])
        if DBG_Z and t_zdbg:
            dep += [t_zdbg[-1]]
        tz1 = p.op("vector", lambda ve, k=k: ve.scalar_tensor_tensor(
            out=zscr, in0=diff[:, k * T:(k + 1) * T], scalar=muxcol[:, k:k + 1],
            in1=xnf[:, k * T:(k + 1) * T], op0=ALU.mult, op1=ALU.add), deps=dep)
        tz2 = p.op("vector", lambda ve, k=k: ve.tensor_tensor(
            zscr, diff[:, k * T:(k + 1) * T], zscr, ALU.mult), deps=[tz1])
        tz3 = p.op("vector", lambda ve, k=k: ve.tensor_tensor(
            z_bf, xnf[:, k * T:(k + 1) * T], zscr, ALU.add), deps=[tz2])
        if DBG_Z:
            t_zdbg.append(p.dma("zdbg", lambda s, k=k: s.dma_start(
                out=d["z_dbg"][k * 128:(k + 1) * 128, :], in_=z_bf), deps=[tz3]))
        def f_hmm(te, k=k):
            last = None
            for n in range(4):
                last = te.matmul(PS[0:128, n * 512:(n + 1) * 512], ab_s[k % 2][0:128, 0:128],
                                 z_bf[:, n * 512:(n + 1) * 512],
                                 start=(k == 0), stop=(k == KT - 1))
            for n in range(4):
                last = te.matmul(PS[0:64, 2048 + n * 512:2048 + (n + 1) * 512],
                                 ab_s[k % 2][0:128, 128:192],
                                 z_bf[:, n * 512:(n + 1) * 512],
                                 start=(k == 0), stop=(k == KT - 1))
            return last
        dep = [tz3, t_acv[k]] + ([t_st2] if k == 0 else [t_zmm[-1]])
        t_zmm.append(p.op("tensor", f_hmm, deps=dep))
        if k + 2 < KT:
            emit_ald(k + 2)
    hdn1 = carve(AR3, 53248, T, BF16)
    hdn2 = carve(AR3, 57344, T, BF16)
    t_hdn1 = p.op("scalar", lambda a: a.activation(
        out=hdn1, in_=PS[0:128, 0:2048], func=AF.Tanh, bias=abcol[:, 0:1]),
        deps=[t_zmm[-1], t_sml])
    t_hdn2 = p.op("scalar", lambda a: a.activation(
        out=hdn2[0:64, :], in_=PS[0:64, 2048:4096], func=AF.Tanh, bias=abcol[0:64, 1:2]),
        deps=[t_zmm[-1], t_sml])
    t_hsp1 = p.dma("hsp", lambda s: s.dma_start(out=d["hdn_sp"][0:128, :], in_=hdn1),
                   deps=[t_hdn1])
    t_hsp2 = p.dma("hsp", lambda s: s.dma_start(out=d["hdn_sp"][128:192, :],
                                                in_=hdn2[0:64, :]), deps=[t_hdn2])

    def _trunc(deps):
        td = p.dma("x3t", lambda s: s.dma_start(
            out=d["x3"].rearrange("(kt p) t -> p kt t", p=128),
            in_=xs.rearrange("p (kt t) -> p kt t", kt=KT)), deps=deps)
        p.op("gpsimd", lambda ge: None, deps=[td])
        p.emit()
    if TRUNC_B == 'B':
        _trunc([t_hsp1, t_hsp2])
        return

    bstg = [carve(AR3, 0, 1024, F32), carve(AR3, 4096, 1024, F32)]
    bb_s = [carve(AR3, 8192, 1024, BF16), carve(AR3, 10240, 1024, BF16)]
    hg_s = [carve(AR3, 12288, T, BF16), carve(AR3, 16384, T, BF16)]
    ddb = [carve(AR3, 20480, T, BF16), carve(AR3, 28672, T, BF16)]
    ddscr = carve(AR3, 32768, T, F32)
    evst = [carve(AR3, 40960, T, BF16), carve(AR3, 45056, T, BF16)]
    mhd_s = carve(AR3, 53248, 5120, BF16)
    gw_b = [carve(AR3, 49152, 1024, BF16), carve(AR3, 51200, 1024, BF16)]
    t_mh = []
    t_mz0 = p.op("vector", lambda ve: ve.memset(mhd_s, 0.0),
                 deps=[t_hsp1, t_hsp2, t_xn[-1], t_hdn1, t_hdn2])
    for j in range(5):
        dep = [t_mz0] if j == 0 else [t_mh[-1]]
        def f_ld(s, j=j):
            i1 = s.dma_start(out=bstg[0][0:64, :], in_=d["mhdT"][:, j * 1024:(j + 1) * 1024])
            i2 = s.dma_start(out=bstg[0][64:128, :], in_=d["mhdT"][:, j * 1024:(j + 1) * 1024])
            return [i1, i2]
        t_lj = p.dma("wsm", f_ld, deps=dep, n=2)
        tp_ = t_lj
        for h in range(16):
            hh = h % 2
            tp_ = p.op("vector", lambda ve, j=j, h=h, hh=hh: ve.tensor_copy(
                mhd_s[hh * 64:hh * 64 + 64,
                      (j * 8 + h // 2) * 128 + hh * 64:(j * 8 + h // 2) * 128 + hh * 64 + 64],
                bstg[0][hh * 64:hh * 64 + 64, h * 64:(h + 1) * 64]), deps=[tp_])
        t_mh.append(tp_)
    t_mhall = t_mh[-1]

    t_hg, t_bld, t_bcv, t_dd = {}, {}, {}, {}
    tok_pev, tok_spl, tok_td1 = {}, {}, {}
    PROJ_T = ["k_hbm", "v_hbm", "r_hbm", "w_hbm", "u_hbm"]

    def prev_gm(g, m, k):
        mm = m - k
        gg = g
        while mm < 0:
            gg -= 1
            mm += KT
        return (gg, mm) if gg >= 0 else None

    for g in range(6):
        gs = g % 2
        bs = 4 * gs
        dep = [t_hsp1, t_hsp2]
        if g >= 2:
            dep = [(lambda g=g: t_dd[(g - 2, KT - 1)]), t_hsp1, t_hsp2]
        t_hg[g] = p.dma(f"hg{gs}", lambda s, g=g, gs=gs: s.dma_start(
            out=hg_s[gs][0:32, :], in_=d["hdn_sp"][g * 32:(g + 1) * 32, :]), deps=dep)
        t_bld[g] = p.dma(f"bld{gs}", lambda s, g=g, gs=gs: s.dma_start(
            out=bstg[gs][0:32, :], in_=d["bT"][:, g * 1024:(g + 1) * 1024]),
            deps=dep + [t_mhall])
        t_bcv[g] = p.op("vector", lambda ve, gs=gs: ve.tensor_copy(
            bb_s[gs][0:32, :], bstg[gs][0:32, :]), deps=[t_bld[g], t_mhall])
        for m in range(KT):
            def f_bm(te, g=g, gs=gs, m=m, bs=bs):
                last = None
                for n in range(4):
                    last = te.matmul(PS[:, (bs + n) * 512:(bs + n + 1) * 512],
                                     bb_s[gs][0:32, m * 128:(m + 1) * 128],
                                     hg_s[gs][0:32, n * 512:(n + 1) * 512],
                                     start=True, stop=True)
                return last
            dep = [t_bcv[g], t_hg[g]]
            if g == 0 and m == 0:
                dep += [t_zmm[-1], t_hdn1, t_hdn2]
            if m >= 1:
                dep += [(lambda g=g, m=m: tok_td1[(g, m - 1)])]
            elif g >= 1:
                dep += [(lambda g=g: tok_pev[(g - 1, KT - 1)])]
            tmm = p.op("tensor", f_bm, deps=dep)
            td1 = p.op("vector", lambda ve, g=g, m=m, bs=bs: ve.tensor_scalar(
                out=ddscr, in0=PS[:, bs * 512:(bs + 4) * 512],
                scalar1=lamcol[:, g * 8 + m:g * 8 + m + 1],
                scalar2=b2col[:, g * 8 + m:g * 8 + m + 1],
                op0=ALU.add, op1=ALU.add),
                deps=[tmm, t_sml] +
                ([(lambda pm=prev_gm(g, m, 1): t_dd[pm])] if (g, m) != (0, 0) else []))
            tok_td1[(g, m)] = td1
            td2 = p.op("vector", lambda ve, m=m: ve.tensor_tensor(
                ddscr, diff[:, m * T:(m + 1) * T], ddscr, ALU.mult), deps=[td1])
            dep3 = [td2]
            pm = prev_gm(g, m, 2)
            if pm:
                dep3 += [(lambda pm=pm: tok_spl[pm])]
            t_dd[(g, m)] = p.op("vector", lambda ve, m=m, ms=m % 2: ve.tensor_tensor(
                ddb[ms], xnf[:, m * T:(m + 1) * T], ddscr, ALU.add), deps=dep3)
            if g < 5:
                def f_pm(te, g=g, m=m, ms=m % 2, obs=4 - bs):
                    last = None
                    for n in range(4):
                        last = te.matmul(
                            PS[0:128, (obs + n) * 512:(obs + n + 1) * 512],
                            mhd_s[:, (g * 8 + m) * 128:(g * 8 + m) * 128 + 128],
                            ddb[ms][:, n * 512:(n + 1) * 512],
                            start=True, stop=True)
                    return last
                dep = [t_dd[(g, m)], t_mhall]
                if m >= 1:
                    dep += [(lambda g=g, m=m: tok_pev[(g, m - 1)])]
                elif g >= 1:
                    dep += [(lambda g=g: tok_td1[(g - 1, KT - 1)])]
                tpm = p.op("tensor", f_pm, deps=dep)
                dep = [tpm]
                pm = prev_gm(g, m, 2)
                if pm and pm[0] < 5:
                    dep += [(lambda pm=pm: tok_spl[pm])]
                tok_pev[(g, m)] = p.op("vector", lambda ve, ms=m % 2, obs=4 - bs: ve.tensor_copy(
                    evst[ms], PS[:, obs * 512:(obs + 4) * 512]), deps=dep)
                tok_spl[(g, m)] = p.dma(f"psp{m % 2}", lambda s, g=g, m=m, ms=m % 2: s.dma_start(
                    out=d[PROJ_T[g]][m * 128:(m + 1) * 128, :], in_=evst[ms]),
                    deps=[tok_pev[(g, m)]])
            else:
                dep = [t_dd[(g, m)]]
                pm = prev_gm(g, m, 2)
                dep += [(lambda pm=pm: tok_spl[pm])]
                tok_spl[(g, m)] = p.dma(f"psp{m % 2}", lambda s, m=m, ms=m % 2: s.dma_start(
                    out=d["dd5_sp"][m * 128:(m + 1) * 128, :], in_=ddb[ms]), deps=dep)
                tok_pev[(g, m)] = tok_spl[(g, m)]

    dd5r = AR1.bitcast(BF16)[:, 0:KT * T]
    t_dd5l = p.dma("dd5l", lambda s: s.dma_start(
        out=dd5r.rearrange("p (kt t) -> p kt t", kt=KT),
        in_=d["dd5_sp"].rearrange("(kt p) t -> p kt t", p=128)),
        deps=[tok_spl[(5, KT - 1)], tok_spl[(5, KT - 2)]])
    t_gwl, t_gwc, t_gmm, t_gev, t_gsp = {}, {}, {}, {}, {}
    for m in range(KT):
        ms = m % 2
        dep = [t_bcv[5]] if m < 2 else [t_gwc[m - 2]]
        t_gwl[m] = p.dma(f"gwl{ms}", lambda s, m=m, ms=ms: s.dma_start(
            out=bstg[ms].rearrange("p (kt c) -> p kt c", kt=KT),
            in_=d["gwT"].rearrange("(kt p) c -> p kt c", p=128)[:, :, m * 128:(m + 1) * 128]),
            deps=dep + ([(lambda: t_dd[(5, KT - 1)])] if m < 2 else []))
        t_gwc[m] = p.op("vector", lambda ve, m=m, ms=ms: ve.tensor_copy(
            gw_b[ms], bstg[ms]), deps=[t_gwl[m]] + ([t_gmm[m - 2]] if m >= 2 else []))
        def f_gm(te, m=m, ms=ms):
            last = None
            for n in range(4):
                pb = PS[:, (4 * ms + n) * 512:(4 * ms + n + 1) * 512]
                for k in range(KT):
                    last = te.matmul(pb, gw_b[ms][:, k * 128:(k + 1) * 128],
                                     dd5r[:, k * T + n * 512:k * T + (n + 1) * 512],
                                     start=(k == 0), stop=(k == KT - 1))
            return last
        dep = [t_gwc[m], t_dd5l]
        if m >= 2:
            dep += [t_gev[m - 2]]
        else:
            dep += [(lambda: tok_pev[(4, KT - 1)]), (lambda: t_dd[(5, KT - 1)])]
        t_gmm[m] = p.op("tensor", f_gm, deps=dep)
        t_gev[m] = p.op("scalar", lambda a, m=m, ms=ms: a.activation(
            out=evst[ms], in_=PS[:, 4 * ms * 512:(4 * ms + 4) * 512], func=AF.Silu),
            deps=[t_gmm[m]] + ([t_gsp[m - 2]] if m >= 2 else
                               [(lambda: tok_spl[(4, KT - 1)]),
                                (lambda: tok_spl[(5, KT - 1)])]))
        t_gsp[m] = p.dma(f"gsp{ms}", lambda s, m=m, ms=ms: s.dma_start(
            out=d["g_hbm"][m * 128:(m + 1) * 128, :], in_=evst[ms]), deps=[t_gev[m]])

    if TRUNC_B == 'C':
        _trunc([tok_spl[(5, KT - 1)], tok_spl[(5, KT - 2)],
                (lambda: t_gsp[KT - 1]), (lambda: t_gsp[KT - 2])])
        return

    wq_b = carve(AR2, 16384, T, BF16)
    wq_b = carve(AR2, 0, T, BF16)
    wh_b = carve(AR2, 4096, T, BF16)
    E_t = carve(AR2, 8192, T, F32)
    Lc_t = carve(AR2, 16384, T, F32)
    Lcm1 = carve(AR2, 24576, T, F32)
    zz_t = carve(AR2, 32768, T, F32)
    fexp = carve(AR2, 40960, T, F32)
    fbf = carve(AR2, 49152, T, BF16)
    krl = carve(AR2, 53248, T, BF16)
    spt = carve(AR2, 57344, T, BF16)
    wkva_s = carve(AR2, 61440, 1024, BF16)
    wkvb_s = carve(AR2, 63488, 1024, BF16)
    DEC_sb = carve(AR3, 0, 2048, F32)
    wstg2 = carve(AR3, 4096, 1024, F32)
    def _bd_stage(dst, hbm, dep0):
        tz = p.op("vector", lambda ve, dst=dst: ve.memset(dst, 0.0), deps=dep0)
        def f_l(s, hbm=hbm):
            i1 = s.dma_start(out=wstg2[0:64, :], in_=d[hbm])
            i2 = s.dma_start(out=wstg2[64:128, :], in_=d[hbm])
            return [i1, i2]
        tl = p.dma("wsm", f_l, deps=dep0, n=2)
        dl_ = [tl, tz]
        for h in range(16):
            hh = h % 2
            tok = p.op("vector", lambda ve, dst=dst, h=h, hh=hh: ve.tensor_copy(
                dst[hh * 64:hh * 64 + 64,
                    (h // 2) * 128 + hh * 64:(h // 2) * 128 + hh * 64 + 64],
                wstg2[hh * 64:hh * 64 + 64, h * 64:(h + 1) * 64]), deps=dl_)
            dl_ = [tok]
        return tok
    t_wkab = _bd_stage(wkva_s, "wkvaT",
                       [(lambda: t_dd[(5, KT - 1)]), (lambda: t_gwc[KT - 1]),
                        (lambda: t_gev[KT - 1])])
    t_wkbb = _bd_stage(wkvb_s, "wkvbT", [t_wkab])
    t_zz = p.op("vector", lambda ve: ve.memset(zz_t, 0.0),
                deps=[(lambda: t_dd[(5, KT - 1)]), (lambda: t_gev[KT - 1])])
    tok_fold = {}
    prev = {}
    for kt in range(KT):
        dep = [tok_spl[(3, kt)], t_zz, t_wkbb, (lambda: t_gev[KT - 1])]
        if kt >= 1:
            dep += [prev["whm"]]
        t_wl = p.dma("wrl", lambda s, kt=kt: s.dma_start(
            out=wq_b, in_=d["w_hbm"][kt * 128:(kt + 1) * 128, :]), deps=dep)
        def f_whmm(te, kt=kt):
            last = None
            for n in range(4):
                last = te.matmul(PS[0:128, n * 512:(n + 1) * 512],
                                 wkva_s[:, kt * 128:(kt + 1) * 128],
                                 wq_b[:, n * 512:(n + 1) * 512],
                                 start=True, stop=True)
            return last
        dep = [t_wl, t_wkbb]
        if kt >= 1:
            dep += [prev["wh"]]
        if kt == 0:
            dep += [(lambda: t_gmm[KT - 1])]
        t_whm = p.op("tensor", f_whmm, deps=dep)
        dep = [t_whm] + ([prev["w2m"]] if kt >= 1 else [])
        t_wh = p.op("scalar", lambda a: a.activation(
            out=wh_b, in_=PS[:, 0:2048], func=AF.Tanh), deps=dep)
        def f_w2mm(te, kt=kt):
            last = None
            for n in range(4):
                last = te.matmul(PS[0:128, 2048 + n * 512:2048 + (n + 1) * 512],
                                 wkvb_s[:, kt * 128:(kt + 1) * 128],
                                 wh_b[:, n * 512:(n + 1) * 512],
                                 start=True, stop=True)
            return last
        dep = [t_wh] + ([prev["E"]] if kt >= 1 else [])
        t_w2m = p.op("tensor", f_w2mm, deps=dep)
        t_E = p.op("scalar", lambda a, kt=kt: a.activation(
            out=E_t, in_=PS[:, 2048:4096], func=AF.Exp, bias=wlamcol[:, kt:kt + 1]),
            deps=[t_w2m, t_sml] + ([prev["lc"]] if kt >= 1 else []))
        dep = [t_E] + ([prev["wds"]] if kt >= 1 else [])
        t_wd = p.op("scalar", lambda a: a.activation(
            out=fbf, in_=E_t, func=AF.Exp, scale=-1.0), deps=dep)
        t_wds = p.dma("wds", lambda s, kt=kt: s.dma_start(
            out=d["wd_hbm"][kt * 128:(kt + 1) * 128, :], in_=fbf), deps=[t_wd])
        t_lc = p.op("vector", lambda ve: ve.tensor_tensor_scan(
            out=Lc_t, data0=E_t, data1=zz_t, initial=0.0, op0=ALU.add, op1=ALU.add),
            deps=[t_E, t_zz] + ([prev["q1"]] if kt >= 1 else []))
        t_lm0 = p.op("vector", lambda ve: ve.memset(Lcm1[:, 0:1], 0.0),
                     deps=[t_lc] + ([prev["d1"]] if kt >= 1 else []))
        t_lm = p.op("vector", lambda ve: ve.tensor_copy(Lcm1[:, 1:T], Lc_t[:, 0:T - 1]),
                    deps=[t_lm0])
        v_base = Lcm1.rearrange("p (sc j) -> p sc j", j=8)[:, :, 0:1].broadcast_to((128, 256, 8))
        t_q1 = p.op("vector", lambda ve, v_base=v_base: ve.tensor_tensor(
            fexp.rearrange("p (sc j) -> p sc j", j=8), v_base,
            Lcm1.rearrange("p (sc j) -> p sc j", j=8), ALU.subtract), deps=[t_lm])
        t_q2 = p.op("scalar", lambda a: a.activation(out=fbf, in_=fexp, func=AF.Exp),
                    deps=[t_q1, t_wds])
        t_qs = p.dma("qds", lambda s, kt=kt: s.dma_start(
            out=d["qd_hbm"][kt * 128:(kt + 1) * 128, :], in_=fbf), deps=[t_q2])
        dep = [tok_spl[(2, kt)]] + ([prev["kh"]] if kt >= 1 else [])
        t_rl = p.dma("krl", lambda s, kt=kt: s.dma_start(
            out=krl, in_=d["r_hbm"][kt * 128:(kt + 1) * 128, :]), deps=dep)
        t_rt = p.op("vector", lambda ve: ve.tensor_tensor(spt, fbf, krl, ALU.mult),
                    deps=[t_rl, t_q2] + ([prev["khs"]] if kt >= 1 else []))
        t_rts = p.dma("rts", lambda s, kt=kt: s.dma_start(
            out=d["rt_hbm"][kt * 128:(kt + 1) * 128, :], in_=spt), deps=[t_rt])
        v_end = Lc_t.rearrange("p (sc j) -> p sc j", j=8)[:, :, 7:8].broadcast_to((128, 256, 8))
        t_k1 = p.op("vector", lambda ve, v_end=v_end: ve.tensor_tensor(
            fexp.rearrange("p (sc j) -> p sc j", j=8),
            Lc_t.rearrange("p (sc j) -> p sc j", j=8), v_end, ALU.subtract),
            deps=[t_lc, t_q2, t_rt])
        t_k2 = p.op("scalar", lambda a: a.activation(out=fbf, in_=fexp, func=AF.Exp),
                    deps=[t_k1, t_qs])
        t_kl = p.dma("krl", lambda s, kt=kt: s.dma_start(
            out=krl, in_=d["k_hbm"][kt * 128:(kt + 1) * 128, :]),
            deps=[tok_spl[(0, kt)], t_rt])
        t_kh = p.op("vector", lambda ve: ve.tensor_tensor(spt, fbf, krl, ALU.mult),
                    deps=[t_kl, t_k2, t_rts])
        t_khs = p.dma("khs", lambda s, kt=kt: s.dma_start(
            out=d["kh_hbm"][kt * 128:(kt + 1) * 128, :], in_=spt), deps=[t_kh])
        s_base = Lcm1.rearrange("p (sc j) -> p sc j", j=8)[:, :, 0]
        s_end = Lc_t.rearrange("p (sc j) -> p sc j", j=8)[:, :, 7]
        t_d1 = p.op("vector", lambda ve, s_base=s_base, s_end=s_end: ve.tensor_tensor(
            fexp[:, 0:256], s_base, s_end, ALU.subtract), deps=[t_lc, t_lm, t_k1])
        tok_fold[kt] = p.op("scalar", lambda a, kt=kt: a.activation(
            out=DEC_sb[:, kt * 256:(kt + 1) * 256], in_=fexp[:, 0:256], func=AF.Exp),
            deps=[t_d1, t_k2])
        prev = {"whm": t_whm, "wh": t_wh, "w2m": t_w2m, "E": t_E, "lc": t_lc,
                "q1": t_q1, "d1": t_d1, "wds": t_wds, "kh": t_kh, "khs": t_khs,
                "qs": t_qs, "rts": t_rts}
    spill_fin = [prev["khs"], prev["wds"], prev["qs"], prev["rts"]]
    if TRUNC_B == 'D':
        _trunc(spill_fin + [tok_fold[KT - 1], tok_spl[(5, KT - 1)],
                            tok_spl[(5, KT - 2)], t_gsp[KT - 1], t_gsp[KT - 2]])
        return

    QW = 4096
    kq = carve(AR1, 0, QW, BF16)
    vq = carve(AR1, 8192, QW, BF16)
    rq = carve(AR1, 16384, QW, BF16)
    uq = carve(AR1, 24576, QW, BF16)
    qdq = carve(AR1, 40960, QW, BF16)
    wdq = carve(AR1, 49152, QW, BF16)
    Pq = carve(AR1, 57344, QW, BF16)
    rtq2 = carve(AR1, 49152, 8192, BF16)
    khq = carve(AR2, 0, QW, BF16)
    ktmq = carve(AR2, 8192, QW, BF16)
    vtmq = carve(AR2, 16384, QW, BF16)
    acc = carve(AR2, 24576, QW, F32)
    gnrow = carve(AR2, 40960, 2048, F32)
    mu2_b = carve(AR2, 49152, 512, F32)
    rs2_b = carve(AR2, 51200, 512, F32)
    ow_stg = carve(AR2, 53248, 1024, F32)
    ow_b = [carve(AR2, 57344, 1024, BF16), carve(AR2, 59392, 1024, BF16)]
    x3st = [carve(AR2, 61440, 512, F32), carve(AR2, 63488, 512, F32)]
    gq = carve(AR3, 8192, QW, BF16)
    scr1 = carve(AR3, 16384, QW, BF16)
    scr2 = carve(AR3, 24576, QW, BF16)
    vrq = carve(AR3, 32768, QW, BF16)
    outn = vrq
    mrow = carve(AR3, 40960, 512, BF16)
    ident_s = carve(AR3, 41984, 128, F32)
    S_ALL = carve(AR3, 43008, 512, F32)
    S_T2 = carve(AR3, 45056, 1024, BF16)
    xrl2 = carve(AR3, 47104, 512, F32)
    tmsl = [carve(AR3, 49152, 4096, BF16), carve(AR3, 57344, 4096, BF16)]
    t_id = p.dma("sml2", lambda s: s.dma_start(out=ident_s, in_=d["ident"]),
                 deps=[(lambda: tok_fold[KT - 1]), (lambda: t_gsp[KT - 1]),
                       (lambda: t_gsp[KT - 2])])
    t_s0 = p.op("vector", lambda ve: ve.memset(S_ALL, 0.0),
                deps=[(lambda: tok_fold[KT - 1]), (lambda: t_gsp[KT - 1]),
                      (lambda: t_gsp[KT - 2])])

    tok_q = {}
    t_acc_prev = None
    t_sprev = t_s0
    t_x3o = []
    t_owc, t_omm, t_oev, t_x3od = {}, {}, {}, {}
    t_ste_prev = None
    for q in range(TRUNC_B if isinstance(TRUNC_B, int) else 4):
        QL = [("k_hbm", kq), ("v_hbm", vq), ("r_hbm", rq), ("u_hbm", uq),
              ("qd_hbm", qdq), ("wd_hbm", wdq), ("kh_hbm", khq), ("g_hbm", gq)]
        tq = []
        _qln = 9
        for f in SKIP_B:
            if f.startswith('ql'):
                _qln = int(f[2:])
        for i, (nm, dst) in enumerate(QL[:_qln]):
            dep = [(lambda: tok_fold[KT - 1]), (lambda: t_gsp[KT - 1]),
                   (lambda: t_gsp[KT - 2]), tok_spl[(5, KT - 1)],
                   tok_spl[(5, KT - 2)]] + spill_fin + (t_zdbg[-1:] if DBG_Z else [])
            if q > 0:
                dep += [tok_q[q - 1]]
            tq.append(p.dma(f"ql{i % 4}", lambda s, nm=nm, dst=dst, q=q: s.dma_start(
                out=dst.rearrange("p (kt t) -> p kt t", kt=KT),
                in_=d[nm].rearrange("(kt p) t -> p kt t", p=128)[:, :, q * 512:(q + 1) * 512]),
                deps=dep))
        t_tp = []
        for tb in range(0 if 'tp' in SKIP_B else 4):
            for kt in range(KT):
                t_tp.append(p.dma(f"tp{(tb * KT + kt) % 3}",
                                  lambda s, tb=tb, kt=kt: s.dma_start_transpose(
                    out=ktmq[:, tb * 1024 + kt * 128:tb * 1024 + (kt + 1) * 128],
                    in_=khq[:, kt * 512 + tb * 128:kt * 512 + (tb + 1) * 128]),
                    deps=list(tq)))
                t_tp.append(p.dma(f"tp{(tb * KT + kt) % 3}",
                                  lambda s, tb=tb, kt=kt: s.dma_start_transpose(
                    out=vtmq[:, tb * 1024 + kt * 128:tb * 1024 + (kt + 1) * 128],
                    in_=vq[:, kt * 512 + tb * 128:kt * 512 + (tb + 1) * 128]),
                    deps=list(tq)))
        t_tpall = ([t_tp[59], t_tp[61], t_tp[63]]
                   if 'tp' not in SKIP_B else [tq[-1]])
        if TRUNC_B in ('QL', f'Q{q}QL'):
            _trunc(list(t_tpall) + list(tq))
            return

        dep0 = list(tq) + ([t_acc_prev] if t_acc_prev else [])
        t_vr = p.op("vector", lambda ve: ve.tensor_tensor(vrq, vq, rq, ALU.mult), deps=dep0)
        def f_md(te):
            last = None
            for kt in range(KT):
                last = te.matmul(PS[0:16, 3584:4096], hsel[:, kt * 16:(kt + 1) * 16],
                                 vrq[:, kt * 512:(kt + 1) * 512],
                                 start=(kt == 0), stop=(kt == KT - 1))
            return last
        t_md = p.op("tensor", f_md, deps=[t_vr, t_sml] +
                    ([(lambda q=q: tok_q[q - 1])] if q > 0 else []))
        t_mz = p.op("vector", lambda ve: ve.memset(mrow[0:16, :], 0.0),
                    deps=[t_md, t_s0])
        t_me = p.op("vector", lambda ve: ve.tensor_copy(mrow[0:16, :], PS[0:16, 3584:4096]),
                    deps=[t_mz])
        t_uk = p.op("vector", lambda ve: ve.tensor_tensor(scr1, uq, kq, ALU.mult), deps=dep0)
        t_accl = None
        for kt in range(KT):
            t_dp = p.op("tensor", lambda te, kt=kt: te.matmul(
                PS[:, (6 + kt % 2) * 512:(7 + kt % 2) * 512],
                dupsel_s[0:16, kt * 128:(kt + 1) * 128], mrow[0:16, :],
                start=True, stop=True), deps=[t_me] + ([t_accl] if t_accl else []))
            t_accl = p.op("vector", lambda ve, kt=kt: ve.tensor_tensor(
                acc[:, kt * 512:(kt + 1) * 512], scr1[:, kt * 512:(kt + 1) * 512],
                PS[:, (6 + kt % 2) * 512:(7 + kt % 2) * 512], ALU.mult),
                deps=[t_dp, t_uk])
        tok_acc = t_accl
        if TRUNC_B in ('IN0', f'Q{q}IN0'):
            _trunc([tok_acc])
            return
        t_mdp = t_md
        t_pp = None
        for dl in range(1, 8):
            if dl == 2:
                for kt in range(KT):
                    t_pp = p.op("vector", lambda ve, kt=kt: ve.tensor_copy(
                        Pq[:, kt * 512 + 1:(kt + 1) * 512], wdq[:, kt * 512:(kt + 1) * 512 - 1]),
                        deps=dep0 + [tok_acc] + ([t_pp] if t_pp else []))
            elif dl > 2:
                for kt in range(KT):
                    t_pp = p.op("vector", lambda ve, kt=kt, dl=dl: ve.tensor_tensor(
                        Pq[:, kt * 512 + dl - 1:(kt + 1) * 512],
                        Pq[:, kt * 512 + dl - 1:(kt + 1) * 512],
                        wdq[:, kt * 512:(kt + 1) * 512 - dl + 1], ALU.mult), deps=[t_pp])
            t_vrd = []
            for kt in range(KT):
                t_vrd.append(p.op("vector", lambda ve, kt=kt, dl=dl: ve.tensor_tensor(
                    vrq[:, kt * 512 + dl:(kt + 1) * 512], vq[:, kt * 512:(kt + 1) * 512 - dl],
                    rq[:, kt * 512 + dl:(kt + 1) * 512], ALU.mult),
                    deps=[t_mdp] + ([t_pp] if t_pp else []) + [tok_acc]))
            t_mdn = p.op("tensor", f_md, deps=t_vrd + [t_me])
            t_mzn = p.op("vector", lambda ve: ve.memset(mrow[0:16, :], 0.0),
                         deps=[t_mdn, tok_acc])
            t_men = p.op("vector", lambda ve, dl=dl: ve.tensor_copy(
                mrow[0:16, :].rearrange("p (sc j) -> p sc j", j=8)[:, :, dl:8],
                PS[0:16, 3584:4096].rearrange("p (sc j) -> p sc j", j=8)[:, :, dl:8]),
                deps=[t_mzn])
            t_mdp = t_mdn
            for kt in range(KT):
                t_dp = p.op("tensor", lambda te, kt=kt: te.matmul(
                    PS[:, (6 + kt % 2) * 512:(7 + kt % 2) * 512],
                    dupsel_s[0:16, kt * 128:(kt + 1) * 128], mrow[0:16, :],
                    start=True, stop=True), deps=[t_men, tok_acc])
                if dl == 1:
                    t_t1 = p.op("vector", lambda ve, kt=kt, dl=dl: ve.tensor_copy(
                        scr1[:, kt * 512 + dl:(kt + 1) * 512],
                        kq[:, kt * 512:(kt + 1) * 512 - dl]), deps=[t_vrd[kt], t_uk])
                else:
                    t_t1 = p.op("vector", lambda ve, kt=kt, dl=dl: ve.tensor_tensor(
                        scr1[:, kt * 512 + dl:(kt + 1) * 512],
                        Pq[:, kt * 512 + dl:(kt + 1) * 512],
                        kq[:, kt * 512:(kt + 1) * 512 - dl], ALU.mult),
                        deps=[t_pp, t_vrd[kt]])
                t_t2 = p.op("vector", lambda ve, kt=kt, dl=dl: ve.tensor_tensor(
                    scr2[:, kt * 512 + dl:(kt + 1) * 512],
                    scr1[:, kt * 512 + dl:(kt + 1) * 512],
                    PS[:, (6 + kt % 2) * 512 + dl:(7 + kt % 2) * 512], ALU.mult),
                    deps=[t_dp, t_t1])
                tok_acc = p.op("vector", lambda ve, kt=kt, dl=dl: ve.tensor_tensor(
                    acc[:, kt * 512 + dl:(kt + 1) * 512],
                    acc[:, kt * 512 + dl:(kt + 1) * 512],
                    scr2[:, kt * 512 + dl:(kt + 1) * 512], ALU.add), deps=[t_t2])

        if TRUNC_B in ('IN', f'Q{q}IN'):
            _trunc([tok_acc])
            return
        def f_rt2(s, q=q):
            il = []
            for hh in range(2):
                il.append(s.dma_start(
                    out=rtq2[0:64, :].rearrange("p (kt hh t) -> p kt hh t", kt=KT, hh=2)[:, :, hh, :],
                    in_=d["r_hbm"].rearrange("(kt hh e) t -> e kt hh t", hh=2, e=64)
                    [:, :, hh, q * 512:(q + 1) * 512]))
            return il
        t_rt2 = p.dma("rt2", f_rt2, deps=[tok_acc], n=2)
        tok_oev = {}
        tok_upd = {}
        tok_ta = {}
        for scg in range(32):
            sg = scg % 2
            tb = scg // 8
            tks = []
            dep = list(t_tpall)
            if scg >= 2:
                dep += [tok_upd[(scg - 2, 1)]]
            if 'k8' not in SKIP_B:
                for j in range(2):
                    r0 = (scg % 8) * 16 + j * 8
                    tks.append(p.dma(f"k8{sg}", lambda s, sg=sg, tb=tb, r0=r0, j=j: s.dma_start(
                        out=tmsl[sg][0:8, j * 2048:j * 2048 + 1024],
                        in_=ktmq[r0:r0 + 8, tb * 1024:(tb + 1) * 1024]), deps=dep))
                    tks.append(p.dma(f"k8{sg}", lambda s, sg=sg, tb=tb, r0=r0, j=j: s.dma_start(
                        out=tmsl[sg][0:8, j * 2048 + 1024:(j + 1) * 2048],
                        in_=vtmq[r0:r0 + 8, tb * 1024:(tb + 1) * 1024]), deps=dep))
                t_k8 = tks[-1]
            else:
                t_k8 = t_tpall[0]
            for j in range(2):
                sc = scg * 2 + j
                def f_st(te):
                    last = None
                    for cb in range(8):
                        last = te.transpose(PS[0:64, 2048 + cb * 128:2048 + (cb + 1) * 128],
                                            S_ALL[:, cb * 64:(cb + 1) * 64], ident_s)
                    return last
                dep = [t_sprev, t_id]
                if t_ste_prev is not None:
                    dep += [t_ste_prev]
                if scg == 0 and j == 0 and q > 0:
                    dep += [tok_q[q - 1]]
                if 'st' not in SKIP_B:
                    t_stp = p.op("tensor", f_st, deps=dep)
                    t_ste = p.op("vector", lambda ve: ve.tensor_copy(
                        S_T2[0:64, :], PS[0:64, 2048:3072]), deps=[t_stp])
                else:
                    t_stp = t_sprev
                    t_ste = t_sprev
                t_ste_prev = t_ste
                ob = (sc % 2) * 512
                def f_out(te, sc=sc, ob=ob):
                    last = None
                    for h in range(16):
                        kt = h // 2
                        hb = 64 * (h % 2)
                        rr = (rtq2[0:64, h * 512 + sc * 8:h * 512 + sc * 8 + 8]
                              .rearrange("p (o t) -> p o t", o=1)
                              .broadcast_to((64, 8, 8)))
                        last = te.matmul(
                            PS[hb:hb + 64, ob + kt * 64:ob + kt * 64 + 64],
                            S_T2[0:64, kt * 128 + hb:kt * 128 + hb + 64],
                            rr,
                            start=True, stop=True)
                    return last
                dep = [t_ste, t_rt2]
                if sc >= 2 and (sc - 2) in tok_ta:
                    dep += [tok_ta[sc - 2]]
                if 'out' not in SKIP_B:
                    t_om = p.op("tensor", f_out, deps=dep)
                else:
                    t_om = t_ste
                def f_dec(ve, sc=sc, q=q):
                    last = None
                    for kt in range(KT):
                        idx = kt * 256 + q * 64 + sc
                        last = ve.tensor_tensor(
                            S_ALL[:, kt * 64:(kt + 1) * 64],
                            S_ALL[:, kt * 64:(kt + 1) * 64],
                            DEC_sb[:, idx:idx + 1].broadcast_to((128, 64)), ALU.mult)
                    return last
                if 'dec' in SKIP_B:
                    t_dec = t_stp
                else:
                    t_dec = p.op("vector", f_dec,
                                 deps=[t_stp, (lambda: tok_fold[KT - 1])])
                ub = 3072 + (sc % 2) * 512
                def f_upd(te, j=j, sg=sg, ub=ub):
                    last = None
                    for h in range(16):
                        kt = h // 2
                        hb = 64 * (h % 2)
                        hcol = kt * 128 + hb
                        last = te.matmul(
                            PS[hb:hb + 64, ub + kt * 64:ub + kt * 64 + 64],
                            tmsl[sg][0:8, j * 2048 + hcol:j * 2048 + hcol + 64],
                            tmsl[sg][0:8, j * 2048 + 1024 + hcol:j * 2048 + 1024 + hcol + 64],
                            start=True, stop=True)
                    return last
                if 'upd' not in SKIP_B and 'k8' not in SKIP_B:
                    t_um = p.op("tensor", f_upd, deps=[t_k8, t_dec, t_om])
                    t_sprev = p.op("vector", lambda ve, ub=ub: ve.tensor_tensor(
                        S_ALL, S_ALL, PS[:, ub:ub + 512], ALU.add), deps=[t_um])
                    tok_upd[(scg, j)] = t_sprev
                else:
                    tok_upd[(scg, j)] = t_dec
                if 'out' not in SKIP_B and 'noevac' not in SKIP_B:
                    ta = p.op("vector", lambda ve, sc=sc, ob=ob: ve.tensor_tensor(
                        scr2[:, 0:64].rearrange("p (kt c) -> p kt c", kt=KT),
                        PS[:, ob:ob + 512].rearrange("p (kt c) -> p kt c", kt=KT)[:, :, 0:8],
                        qdq.rearrange("p (kt t) -> p kt t", kt=KT)[:, :, sc * 8:sc * 8 + 8],
                        ALU.mult), deps=[t_om, tok_acc])
                    tk = p.op("vector", lambda ve, sc=sc: ve.tensor_tensor(
                        acc.rearrange("p (kt t) -> p kt t", kt=KT)[:, :, sc * 8:sc * 8 + 8],
                        acc.rearrange("p (kt t) -> p kt t", kt=KT)[:, :, sc * 8:sc * 8 + 8],
                        scr2[:, 0:64].rearrange("p (kt c) -> p kt c", kt=KT), ALU.add),
                        deps=[ta])
                    tok_ta[sc] = ta
                    tok_acc = tk
            if TRUNC_B in (f'SP{scg}', f'Q{q}SP{scg}'):
                _trunc([tok_acc, tok_upd[(scg, 1)]])
                return

        if TRUNC_B in ('SP', f'Q{q}SP'):
            _trunc([tok_acc, tok_upd[(31, 1)]])
            return
        t_gsq1 = p.op("scalar", lambda a: a.activation(
            out=scr1.bitcast(F32)[:, 0:2048], in_=acc[:, 0:2048], func=AF.Square),
            deps=[tok_acc])
        t_gsq2 = p.op("scalar", lambda a: a.activation(
            out=scr2.bitcast(F32)[:, 0:2048], in_=acc[:, 2048:4096], func=AF.Square),
            deps=[tok_acc, t_gsq1])
        def f_gs1(te):
            last = None
            for kt in range(KT):
                last = te.matmul(PS[0:16, 3072:3584], hself[:, kt * 16:(kt + 1) * 16],
                                 acc[:, kt * 512:(kt + 1) * 512],
                                 start=(kt == 0), stop=(kt == KT - 1))
            return last
        t_gs1 = p.op("tensor", f_gs1, deps=[tok_acc, t_sml])
        def f_gs2(te):
            last = None
            for kt in range(KT):
                src = (scr1 if kt < 4 else scr2).bitcast(F32)[:, (kt % 4) * 512:(kt % 4 + 1) * 512]
                last = te.matmul(PS[0:16, 3584:4096], hself[:, kt * 16:(kt + 1) * 16],
                                 src, start=(kt == 0), stop=(kt == KT - 1))
            return last
        t_gs2 = p.op("tensor", f_gs2, deps=[t_gsq1, t_gsq2, t_gs1])
        GM, GV, GE, GR = 0, 512, 1024, 1536
        t_gn1 = p.op("vector", lambda ve: ve.tensor_scalar_mul(
            gnrow[0:16, GM:GM + 512], PS[0:16, 3072:3584], 1.0 / 64), deps=[t_gs1, t_gs2])
        t_gn2 = p.op("vector", lambda ve: ve.tensor_scalar_mul(
            gnrow[0:16, GE:GE + 512], PS[0:16, 3584:4096], 1.0 / 64), deps=[t_gs2])
        t_gn3 = p.op("vector", lambda ve: ve.tensor_tensor(
            gnrow[0:16, GV:GV + 512], gnrow[0:16, GM:GM + 512], gnrow[0:16, GM:GM + 512],
            ALU.mult), deps=[t_gn1])
        t_gn4 = p.op("vector", lambda ve: ve.tensor_sub(
            gnrow[0:16, GV:GV + 512], gnrow[0:16, GE:GE + 512], gnrow[0:16, GV:GV + 512]),
            deps=[t_gn2, t_gn3])
        t_gn5 = p.op("scalar", lambda a: a.activation(
            out=gnrow[0:16, GR:GR + 512], in_=gnrow[0:16, GV:GV + 512], func=AF.Ln,
            bias=c_gneps[0:16, 0:1]), deps=[t_gn4])
        t_gn6 = p.op("scalar", lambda a: a.activation(
            out=gnrow[0:16, GR:GR + 512], in_=gnrow[0:16, GR:GR + 512], func=AF.Exp,
            scale=c_mhalf[0:16, 0:1]), deps=[t_gn5])
        t_gb1 = p.dma("rows", lambda s, q=q: s.dma_start(
            out=d["rows_hbm"][2:18, 0:512], in_=gnrow[0:16, GM:GM + 512]), deps=[t_gn1, t_gn6])
        t_gb2 = p.dma("rows", lambda s, q=q: s.dma_start(
            out=d["rows_hbm"][18:34, 0:512], in_=gnrow[0:16, GR:GR + 512]), deps=[t_gn6])
        if TRUNC_B in ('GN', f'Q{q}GN'):
            _trunc([t_gb1, t_gb2])
            return
        t_on = {}
        prev_on = None
        for kt in range(KT):
            t_db1 = p.dma("gmu", lambda s, kt=kt: s.dma_start(
                out=mu2_b[0:64, :], in_=d["rows_hbm"][2 + 2 * kt:3 + 2 * kt, 0:512]
                .broadcast_to((64, 512))), deps=[t_gb1, t_gb2] + ([prev_on] if prev_on else []))
            t_db2 = p.dma("gmu", lambda s, kt=kt: s.dma_start(
                out=mu2_b[64:128, :], in_=d["rows_hbm"][3 + 2 * kt:4 + 2 * kt, 0:512]
                .broadcast_to((64, 512))), deps=[t_gb1, t_gb2] + ([prev_on] if prev_on else []))
            t_db3 = p.dma("grs", lambda s, kt=kt: s.dma_start(
                out=rs2_b[0:64, :], in_=d["rows_hbm"][18 + 2 * kt:19 + 2 * kt, 0:512]
                .broadcast_to((64, 512))), deps=[t_gb2] + ([prev_on] if prev_on else []))
            t_db4 = p.dma("grs", lambda s, kt=kt: s.dma_start(
                out=rs2_b[64:128, :], in_=d["rows_hbm"][19 + 2 * kt:20 + 2 * kt, 0:512]
                .broadcast_to((64, 512))), deps=[t_gb2] + ([prev_on] if prev_on else []))
            tn1 = p.op("vector", lambda ve, kt=kt: ve.tensor_tensor(
                scr2.bitcast(F32)[:, 0:512], acc[:, kt * 512:(kt + 1) * 512], mu2_b,
                ALU.subtract), deps=[t_db1, t_db2, t_gsq2])
            tn2 = p.op("vector", lambda ve, kt=kt: ve.tensor_tensor(
                scr2.bitcast(F32)[:, 0:512], scr2.bitcast(F32)[:, 0:512], rs2_b,
                ALU.mult), deps=[t_db3, t_db4, tn1])
            tn3 = p.op("vector", lambda ve, kt=kt: ve.tensor_scalar(
                out=scr2.bitcast(F32)[:, 0:512], in0=scr2.bitcast(F32)[:, 0:512],
                scalar1=gncol[:, kt:kt + 1], scalar2=gnbcol[:, kt:kt + 1],
                op0=ALU.mult, op1=ALU.add), deps=[tn2, t_sml])
            prev_on = p.op("vector", lambda ve, kt=kt: ve.tensor_tensor(
                outn[:, kt * 512:(kt + 1) * 512], scr2.bitcast(F32)[:, 0:512],
                gq[:, kt * 512:(kt + 1) * 512], ALU.mult), deps=[tn3] + list(tq[-1:]))
            t_on[kt] = prev_on
        for m in range(KT):
            ms = m % 2
            dep = [t_on[KT - 1]] if m < 2 else [t_owc[(q, m - 2)]]
            t_owl = p.dma(f"owl{ms}", lambda s, m=m: s.dma_start(
                out=ow_stg.rearrange("p (kt c) -> p kt c", kt=KT),
                in_=d["owT"].rearrange("(kt p) c -> p kt c", p=128)[:, :, m * 128:(m + 1) * 128]),
                deps=dep)
            t_owc[(q, m)] = p.op("vector", lambda ve, m=m, ms=ms: ve.tensor_copy(
                ow_b[ms], ow_stg), deps=[t_owl] + ([(lambda q=q, m=m: t_omm[(q, m - 2)])]
                                                   if m >= 2 else []))
            def f_omm(te, m=m, ms=ms):
                last = None
                for k in range(KT):
                    last = te.matmul(PS[:, ms * 512:(ms + 1) * 512],
                                     ow_b[ms][:, k * 128:(k + 1) * 128],
                                     outn[:, k * 512:(k + 1) * 512],
                                     start=(k == 0), stop=(k == KT - 1))
                return last
            dep = [t_owc[(q, m)], t_on[KT - 1]]
            if m >= 2:
                dep += [(lambda q=q, m=m: t_oev[(q, m - 2)])]
            else:
                dep += [t_gs1, t_gs2]
            t_omm[(q, m)] = p.op("tensor", f_omm, deps=dep)
            dep = [t_on[KT - 1]]
            if m >= 1:
                dep += [(lambda q=q, m=m: t_oev[(q, m - 1)])]
            elif q >= 1:
                dep += [(lambda q=q: t_oev[(q - 1, KT - 1)])]
            t_xrl = p.dma("xr2", lambda s, m=m, q=q: s.dma_start(
                out=xrl2, in_=d["x"][m * 128:(m + 1) * 128, q * 512:(q + 1) * 512]),
                deps=dep)
            if m == 0 and q == 0:
                t_oev = {}
            dep = [t_omm[(q, m)], t_xrl, t_sml]
            if m >= 2:
                dep += [(lambda q=q, m=m: t_x3od[(q, m - 2)])]
            elif q >= 1:
                dep += [(lambda q=q, m=m: t_x3od[(q - 1, KT - 2 + m)])]
            t_oba = p.op("vector", lambda ve, m=m, ms=ms: ve.tensor_scalar(
                out=x3st[ms], in0=PS[:, ms * 512:(ms + 1) * 512],
                scalar1=obcol[:, m:m + 1], scalar2=None, op0=ALU.add), deps=dep)
            t_oev[(q, m)] = p.op("vector", lambda ve, ms=ms: ve.tensor_tensor(
                x3st[ms], x3st[ms], xrl2, ALU.add), deps=[t_oba])
            if m == 0 and q == 0:
                t_x3od = {}
            t_x3od[(q, m)] = p.dma(f"x3o{ms}", lambda s, m=m, q=q, ms=ms: s.dma_start(
                out=d["x3"][m * 128:(m + 1) * 128, q * 512:(q + 1) * 512], in_=x3st[ms]),
                deps=[t_oev[(q, m)]])
            t_x3o.append(t_x3od[(q, m)])
        t_acc_prev = t_oev[(q, KT - 1)]
        tok_q[q] = t_acc_prev

    p.op("gpsimd", lambda ge: None, deps=t_x3o)
    p.emit()
    return nc



def _prep_attn_a_inputs(inputs, c):
    x = np.ascontiguousarray(np.asarray(inputs["L0_attn_a_x"][c]).T.astype(np.float32))
    wqkvT = np.ascontiguousarray(np.asarray(inputs["L0_attn_a_W_qkv"]).T.astype(np.float32))
    wvgsT = np.ascontiguousarray(np.asarray(inputs["L0_attn_a_W_vgs"]).T.astype(np.float32))
    woT = np.ascontiguousarray(np.asarray(inputs["L0_attn_a_W_o"]).T.astype(np.float32))
    W = np.asarray(inputs["L0_attn_a_W_LTM"]).astype(np.float32)
    pp = np.arange(128)
    wcols = np.zeros((128, 512), np.float32)
    for h in range(H):
        for b_ in range(NBLK):
            wcols[:, h * 32 + b_] = W[h, 2 * b_ + pp // 64, pp % 64]
    wji = np.zeros((64, 1024), np.float32)
    for h in range(H):
        wji[:, h * 64:(h + 1) * 64] = W[h].T
    prow = np.concatenate([
        np.asarray(inputs["L0_attn_a_beta_tau"]),
        np.asarray(inputs["L0_attn_a_beta_gm"]),
        np.asarray(inputs["L0_attn_a_C_ch"]),
        np.asarray(inputs["L0_attn_a_gamma"]),
        np.asarray(inputs["L0_attn_a_V_T0"]),
        np.asarray(inputs["L0_attn_a_V_gs_base"]),
    ]).astype(np.float32).reshape(1, 96)
    psc = np.array([[np.asarray(inputs["L0_attn_a_alpha_ppd"]).ravel()[0],
                     np.asarray(inputs["L0_attn_a_IC_threshold"]).ravel()[0]]], np.float32)
    return {"x": x, "wqkvT": wqkvT, "wvgsT": wvgsT, "woT": woT,
            "wcols": wcols, "wji": wji, "prow": prow, "psc": psc}


def _prep_ffn_inputs(inputs, c, x1_full):
    sl = x1_full[:, c * 256:(c + 1) * 256, :]
    x = np.ascontiguousarray(sl.transpose(2, 0, 1).reshape(D, 8 * 256))
    out = {"x": x.astype(np.float32)}
    for nm, key in [("wfT", "L0_ffn_wf"), ("wiT", "L0_ffn_wi"), ("wcT", "L0_ffn_wc"),
                    ("ufT", "L0_ffn_uf"), ("uiT", "L0_ffn_ui"), ("ucT", "L0_ffn_uc"),
                    ("woT", "L0_ffn_wo"), ("uoT", "L0_ffn_uo")]:
        out[nm] = np.ascontiguousarray(np.asarray(inputs[key]).T.astype(np.float32))
    bc = np.zeros((128, 32), np.float32)
    for g, key in enumerate(["L0_ffn_bf", "L0_ffn_bi", "L0_ffn_bc", "L0_ffn_bo"]):
        bc[:, g * 8:(g + 1) * 8] = np.asarray(inputs[key]).reshape(8, 128).T
    out["bcol"] = bc
    out["incol"] = np.ascontiguousarray(
        np.asarray(inputs["L0_ffn_in_c"]).reshape(8, 128).T.astype(np.float32))
    return out


def _prep_attn_b_inputs(inputs, c, x2_full):
    x = np.ascontiguousarray(np.asarray(x2_full[c]).T.astype(np.float32))
    I = lambda k: np.asarray(inputs[k]).astype(np.float32)
    out = {"x": x}
    out["muxcol"] = np.ascontiguousarray(I("L0_attn_b_mu_x").reshape(8, 128).T)
    aw = I("L0_attn_b_lora_A_w")
    out["aT"] = np.ascontiguousarray(aw.transpose(1, 0, 2).reshape(1024, 192))
    abf = I("L0_attn_b_lora_A_b").reshape(192)
    abc = np.zeros((128, 2), np.float32)
    abc[:, 0] = abf[0:128]
    abc[0:64, 1] = abf[128:192]
    out["abcol"] = abc
    bw = I("L0_attn_b_lora_B_w")
    out["bT"] = np.ascontiguousarray(bw.transpose(1, 0, 2).reshape(32, 6 * 1024, order="F")
                                     if False else bw.transpose(1, 0, 2)
                                     .transpose(0, 1, 2).reshape(32, -1))
    out["bT"] = np.ascontiguousarray(bw.transpose(1, 0, 2).reshape(32, 6144))
    out["lamcol"] = np.ascontiguousarray(
        I("L0_attn_b_lora_lambda").reshape(6, 8, 128).transpose(2, 0, 1).reshape(128, 48))
    out["b2col"] = np.ascontiguousarray(
        I("L0_attn_b_lora_B_b").reshape(6, 8, 128).transpose(2, 0, 1).reshape(128, 48))
    mh = I("L0_attn_b_mhd_w")
    out["mhdT"] = np.ascontiguousarray(mh.transpose(2, 0, 1, 3).reshape(64, 5 * 16 * 64))
    out["gwT"] = np.ascontiguousarray(I("L0_attn_b_gate_w"))
    wa = I("L0_attn_b_wkv_A")
    out["wkvaT"] = np.ascontiguousarray(wa.transpose(1, 0, 2).reshape(64, 1024))
    wb = I("L0_attn_b_wkv_B")
    out["wkvbT"] = np.ascontiguousarray(wb.transpose(1, 0, 2).reshape(64, 1024))
    out["wlamcol"] = np.ascontiguousarray(I("L0_attn_b_wkv_lamb").reshape(8, 128).T)
    out["gncol"] = np.ascontiguousarray(I("L0_attn_b_gn_gamma").reshape(8, 128).T)
    out["gnbcol"] = np.ascontiguousarray(I("L0_attn_b_gn_beta").reshape(8, 128).T)
    out["owT"] = np.ascontiguousarray(I("L0_attn_b_out_w"))
    out["obcol"] = np.ascontiguousarray(I("L0_attn_b_out_b").reshape(8, 128).T)
    return out


_NC_CACHE = {}


def _get_nc(name, builder):
    if name not in _NC_CACHE:
        _NC_CACHE[name] = builder()
    return _NC_CACHE[name]


def kernel(**inputs):
    nc = _get_nc("attn_a", build_attn_a)
    im = [_prep_attn_a_inputs(inputs, c) for c in range(8)]
    res = run_bass_kernel_spmd(nc, im, core_ids=list(range(8)))
    x1 = np.stack([np.asarray(res.results[c]["x1"]).T for c in range(8)]).astype(np.float32)
    nc = _get_nc("ffn", build_ffn)
    im = [_prep_ffn_inputs(inputs, c, x1) for c in range(8)]
    res = run_bass_kernel_spmd(nc, im, core_ids=list(range(8)))
    x2 = np.stack([np.asarray(res.results[c]["x2"]).reshape(1024, 8, 256).transpose(1, 2, 0)
                   for c in range(8)], axis=1).reshape(8, 2048, 1024).astype(np.float32)
    x2 = np.concatenate(
        [np.asarray(res.results[c]["x2"]).reshape(1024, 8, 256).transpose(1, 2, 0)
         for c in range(8)], axis=1).astype(np.float32)
    nc = _get_nc("attn_b", build_attn_b)
    im = [_prep_attn_b_inputs(inputs, c, x2) for c in range(8)]
    res = run_bass_kernel_spmd(nc, im, core_ids=list(range(8)))
    x3 = np.stack([np.asarray(res.results[c]["x3"]).T for c in range(8)]).astype(np.float32)
    nc = _get_nc("ffn", build_ffn)
    im = [_prep_ffn_inputs(inputs, c, x3) for c in range(8)]
    res = run_bass_kernel_spmd(nc, im, core_ids=list(range(8)))
    x4 = np.concatenate(
        [np.asarray(res.results[c]["x2"]).reshape(1024, 8, 256).transpose(1, 2, 0)
         for c in range(8)], axis=1).astype(np.float32)
    return x4
